# version 17
# baseline (speedup 1.0000x reference)
import numpy as np
import ml_dtypes
from contextlib import ExitStack

import concourse.bass as bass
import concourse.mybir as mybir
from concourse.bass_utils import run_bass_kernel_spmd

F32 = mybir.dt.float32
BF16 = mybir.dt.bfloat16
AF = mybir.ActivationFunctionType
ALU = mybir.AluOpType
AX = mybir.AxisListType

D = 2048
T = 2048
KC = 16
H = 16
HD = 128
E = 8
TG = 16
NCORE = 8
NEG = -1.0e30
QSCALE = float(np.float32(HD ** -0.5))
GOLDEN_CENTER = float(np.float32(1.0 / np.e))
GOLDEN_LOWER = float(np.float32(0.5 - np.log(4.0 / 3.0)))
GOLDEN_UPPER = 0.5
LN_EPS = 1e-5

NTILE = 2
TT = 1024
NHALF = 2
FT = 128
DT = 512


def _host_consts():
    inv_freq = (1.0 / (10000.0 ** (np.arange(0, HD, 2, dtype=np.float64) / HD))).astype(np.float32)
    pos = np.arange(T, dtype=np.float32)
    freqs = pos[None, :] * inv_freq[:, None]
    cos = np.cos(freqs).astype(np.float32)
    sin = np.sin(freqs).astype(np.float32)
    cos2 = np.concatenate([cos, cos], 0).astype(ml_dtypes.bfloat16)
    sin2 = np.concatenate([sin, sin], 0).astype(ml_dtypes.bfloat16)
    sgn = np.concatenate([np.full((64, 1), -1.0, np.float32),
                          np.full((64, 1), 1.0, np.float32)], 0)
    ident = np.eye(128, dtype=np.float32).astype(ml_dtypes.bfloat16)
    r = np.arange(128)[:, None]
    c = np.arange(128)[None, :]
    blocks = []
    for u in range(15):
        o = 11 - u
        if o == 8:
            m = np.where(c > r, 0.0, NEG)
        elif o == 0:
            m = np.where(c <= r, 0.0, NEG)
        elif 1 <= o <= 7:
            m = np.zeros((128, 128))
        else:
            m = np.full((128, 128), NEG)
        blocks.append(m)
    maskt = np.concatenate(blocks, 1).astype(np.float32).astype(ml_dtypes.bfloat16)
    return cos2, sin2, sgn, ident, maskt


class Sem:
    __slots__ = ("name", "n", "h", "max_waited")

    def __init__(self, name):
        self.name = name
        self.n = 0
        self.h = None
        self.max_waited = 0


class Sched:
    ENGINES = ("tensor", "vector", "scalar", "gpsimd", "sync")

    def __init__(self):
        self.q = {e: [] for e in self.ENGINES}
        self._lastw = {e: {} for e in self.ENGINES}
        self.sems = {}

    def sem(self, name):
        s = Sem(name)
        self.sems[name] = s
        return s

    def op(self, eng, fn, waits=(), inc=None, inck=1):
        wl = []
        for item in waits:
            if item is None:
                continue
            if len(item) == 3:
                s, tok, force = item
            else:
                s, tok = item
                force = False
            if s is None or tok is None or tok <= 0:
                continue
            lw = self._lastw[eng].get(s.name, 0)
            if force or tok > lw:
                tok2 = max(tok, lw)
                wl.append((s, tok2))
                self._lastw[eng][s.name] = tok2
                s.max_waited = max(s.max_waited, tok2)
        tok_out = None
        if inc is not None:
            inc.n += inck
            tok_out = inc.n
            self.q[eng].append((fn, wl, (inc, inck)))
        else:
            self.q[eng].append((fn, wl, None))
        return tok_out

    def check(self):
        for s in self.sems.values():
            assert s.max_waited <= s.n, f"sem {s.name}: waits {s.max_waited} > incs {s.n} (hang)"

    def replay(self, eng_name, eng):
        for fn, wl, inc in self.q[eng_name]:
            for s, tok in wl:
                eng.wait_ge(s.h, tok)
            instr = fn(eng)
            if inc is not None:
                s, k = inc
                instr.then_inc(s.h, k)


def build_nc(taps=(), upto="all"):
    nc = bass.Bass("TRN2", target_bir_lowering=False)
    S = Sched()

    x0_d = nc.dram_tensor("x0", [T, D], F32, kind="ExternalInput").ap()
    wq_d = nc.dram_tensor("wq", [D, D], F32, kind="ExternalInput").ap()
    wk_d = nc.dram_tensor("wk", [D, D], F32, kind="ExternalInput").ap()
    wv_d = nc.dram_tensor("wv", [D, D], F32, kind="ExternalInput").ap()
    wo_d = nc.dram_tensor("wo", [D, D], F32, kind="ExternalInput").ap()
    wg_d = nc.dram_tensor("wg3", [E * D, D], F32, kind="ExternalInput").ap()
    wu_d = nc.dram_tensor("wu3", [E * D, D], F32, kind="ExternalInput").ap()
    wd_d = nc.dram_tensor("wd3", [E * D, D], F32, kind="ExternalInput").ap()
    wr_d = nc.dram_tensor("wr", [D, E], F32, kind="ExternalInput").ap()
    temp_d = nc.dram_tensor("tempcol", [128, 1], F32, kind="ExternalInput").ap()
    cos_d = nc.dram_tensor("cos2", [128, T], BF16, kind="ExternalInput").ap()
    sin_d = nc.dram_tensor("sin2", [128, T], BF16, kind="ExternalInput").ap()
    sgn_d = nc.dram_tensor("sgncol", [128, 1], F32, kind="ExternalInput").ap()
    id_d = nc.dram_tensor("identb", [128, 128], BF16, kind="ExternalInput").ap()
    msk_d = nc.dram_tensor("maskt", [128, 1920], BF16, kind="ExternalInput").ap()
    kcol_d = nc.dram_tensor("kcol", [128, 4], F32, kind="ExternalInput").ap()
    out_d = nc.dram_tensor("out", [T, D], F32, kind="ExternalOutput").ap()
    ybuf_d = nc.dram_tensor("ybuf", [T, D], F32, kind="Internal").ap()
    x1f_d = nc.dram_tensor("x1f", [T, D], F32, kind="Internal").ap()
    tap_d = {}
    if "x1t" in taps:
        tap_d["x1t"] = nc.dram_tensor("tap_x1t", [128, KC * T], BF16, kind="ExternalOutput").ap()
    if "outh" in taps:
        tap_d["outh"] = nc.dram_tensor("tap_outh", [128, H * T], BF16, kind="ExternalOutput").ap()
    if "router" in taps:
        tap_d["router"] = nc.dram_tensor("tap_router", [128, 128], F32, kind="ExternalOutput").ap()
    if "x1" in taps:
        tap_d["x1"] = nc.dram_tensor("tap_x1", [T, D], F32, kind="ExternalOutput").ap()
    if "y" in taps:
        tap_d["y"] = nc.dram_tensor("tap_y", [T, D], F32, kind="ExternalOutput").ap()
    if "x0t" in taps:
        tap_d["x0t"] = nc.dram_tensor("tap_x0t", [128, KC * T], BF16, kind="ExternalOutput").ap()
    if "qkv" in taps:
        tap_d["qt"] = nc.dram_tensor("tap_qt", [128, T], BF16, kind="ExternalOutput").ap()
        tap_d["kt"] = nc.dram_tensor("tap_kt", [128, T], BF16, kind="ExternalOutput").ap()
        tap_d["vt"] = nc.dram_tensor("tap_vt", [128, T], BF16, kind="ExternalOutput").ap()
    if "probs" in taps:
        tap_d["probs"] = nc.dram_tensor("tap_probs", [128, 9216], BF16, kind="ExternalOutput").ap()
        tap_d["probst"] = nc.dram_tensor("tap_probst", [128, 6144], BF16, kind="ExternalOutput").ap()
        tap_d["stat"] = nc.dram_tensor("tap_stat", [128, 32], F32, kind="ExternalOutput").ap()

    s_pe = S.sem("pe"); s_act = S.sem("act"); s_dve = S.sem("dve")
    s_bulk = S.sem("bulk")
    s_gbulk = S.sem("gbulk")
    s_pool = S.sem("pool")
    s_wsl = [S.sem(f"wsl{i}") for i in range(4)]
    s_wgwu = [S.sem(f"wgwu{i}") for i in range(2)]
    s_wd = [S.sem(f"wd{i}") for i in range(2)]
    s_wo = S.sem("wo")
    s_x0s = [S.sem(f"x0s{i}") for i in range(4)]
    s_yw = S.sem("yw")
    s_yr = [S.sem(f"yr{i}") for i in range(2)]
    s_x1w = S.sem("x1w")
    s_x1r = S.sem("x1r")
    s_out = S.sem("sout")

    def pe(fn, waits=(), inc=False):
        return S.op("tensor", fn, waits, s_pe if inc else None)

    def dve(fn, waits=(), inc=False):
        w = list(waits) + [(s_dve, s_dve.n, True)]
        return S.op("vector", fn, w, s_dve)

    def act(fn, waits=(), inc=False):
        w = list(waits) + [(s_act, s_act.n, True)]
        return S.op("scalar", fn, w, s_act)

    def pool(fn, waits=()):
        w = list(waits) + [(s_pool, s_pool.n, True)]
        return S.op("gpsimd", fn, w, s_pool)

    last_dma = {}

    def _dma(eng, dst, src, sem, waits, tkey):
        w = list(waits)
        if tkey is None:
            tkey = dst.tensor.name if hasattr(dst, "tensor") else None
        if tkey is not None and tkey in last_dma:
            ps = last_dma[tkey]
            w.append((ps, ps.n, True))
        tok = S.op(eng, (lambda g, d=dst, s=src: g.dma_start(d, s)), w, sem, 16)
        if tkey is not None:
            last_dma[tkey] = sem
        return tok

    def gdma(dst, src, sem, waits=(), tkey=None):
        return _dma("gpsimd", dst, src, sem, waits, tkey)

    def sdma(dst, src, sem, waits=(), tkey=None):
        return _dma("sync", dst, src, sem, waits, tkey)

    with ExitStack() as ctx:
        for s in S.sems.values():
            s.h = ctx.enter_context(nc.semaphore(s.name))

        xT = ctx.enter_context(nc.sbuf_tensor("xT", [128, KC * T], BF16))
        identb = ctx.enter_context(nc.sbuf_tensor("identb_s", [128, 128], BF16))
        wrb = ctx.enter_context(nc.sbuf_tensor("wrb", [128, KC * E], BF16))
        rtr = ctx.enter_context(nc.sbuf_tensor("rtr", [128, 288], F32))
        stat = ctx.enter_context(nc.sbuf_tensor("stat", [128, 64], F32))
        sgncol = ctx.enter_context(nc.sbuf_tensor("sgncol_s", [128, 1], F32))
        tempcol = ctx.enter_context(nc.sbuf_tensor("tempcol_s", [128, 1], F32))
        kcol = ctx.enter_context(nc.sbuf_tensor("kcol_s", [128, 4], F32))
        ctxA = ExitStack()
        ctx.callback(ctxA.close)
        big = ctxA.enter_context(nc.sbuf_tensor("big", [128, H * T], BF16))

        psAB = [ctx.enter_context(nc.psum_tensor(n, [128, 1536], F32)) for n in ("psA", "psB")]

        def psx(par, reg):
            return psAB[par][:, reg * 512:(reg + 1) * 512]

        ps = [psx(0, 0), psx(1, 0), psx(0, 1), psx(1, 1), psx(0, 2), psx(1, 2)]
        ptr = ctx.enter_context(nc.psum_tensor("ptr", [128, 1024], BF16))
        psav = ctx.enter_context(nc.psum_tensor("psav", [128, 512], F32))

        bank_free = {}

        def mm_group(psum_ap, pairs, extra_waits=(), bank=None):
            n = len(pairs)
            tok = None
            for i, (l, r) in enumerate(pairs):
                w = list(extra_waits) if i == 0 else []
                if i == 0 and bank is not None and bank in bank_free:
                    bf = bank_free[bank]
                    w.extend(bf if isinstance(bf, list) else [bf])
                tok = S.op("tensor",
                           (lambda t, o=psum_ap, a=l, b=r, st=(i == 0), sp=(i == n - 1):
                            t.matmul(o, a, b, start=st, stop=sp)),
                           w, s_pe if i == n - 1 else None)
            return tok

        sdma(identb[:, :], id_d[:, :], s_bulk)
        sdma(sgncol[:, :], sgn_d[:, :], s_bulk)
        sdma(tempcol[:, :], temp_d[:, :], s_bulk)
        bigv = big.ap().rearrange("p (tg d) -> p tg d", d=D)
        x0v = x0_d.rearrange("(tg p) d -> p tg d", p=128)
        gdma(bigv[:, 0:8], x0v[:, 0:8], s_gbulk)
        gdma(bigv[:, 8:16], x0v[:, 8:16], s_gbulk)
        gdma(wrb.ap().rearrange("p (kc e) -> p kc e", e=E),
             wr_d.rearrange("(kc p) e -> p kc e", p=128), s_gbulk)
        sdma(kcol[:, :], kcol_d[:, :], s_bulk)

        ctxH = ExitStack()
        ctx.callback(ctxH.close)
        cos2 = ctxH.enter_context(nc.sbuf_tensor("cos2_s", [128, T], BF16))
        sin2 = ctxH.enter_context(nc.sbuf_tensor("sin2_s", [128, T], BF16))
        maskt = ctxH.enter_context(nc.sbuf_tensor("maskt_s", [128, 1920], BF16))
        qT = ctxH.enter_context(nc.sbuf_tensor("qT", [128, T], BF16))
        kT = ctxH.enter_context(nc.sbuf_tensor("kT", [128, T], BF16))
        vT = ctxH.enter_context(nc.sbuf_tensor("vT", [128, T], BF16))
        probs = ctxH.enter_context(nc.sbuf_tensor("probs", [128, 9216], BF16))
        probsT = ctxH.enter_context(nc.sbuf_tensor("probsT", [128, 6144], BF16))
        ropet = ctxH.enter_context(nc.sbuf_tensor("ropet", [128, 1024], BF16))
        zerob = ctxH.enter_context(nc.sbuf_tensor("zerob", [128, 128], BF16))
        wsl = [ctxH.enter_context(nc.sbuf_tensor(f"wsl{i}", [128, KC * 128], BF16))
               for i in range(4)]

        sdma(cos2[:, :], cos_d[:, :], s_bulk)
        sdma(sin2[:, :], sin_d[:, :], s_bulk)
        sdma(maskt[:, :], msk_d[:, :], s_bulk)
        tok_z = dve(lambda v, o=zerob[:, :]: v.memset(o, 0.0), [], inc=True)
        bulk_w = [(s_bulk, s_bulk.n), (s_gbulk, s_gbulk.n), (s_dve, tok_z)]

        xTv = xT.ap().rearrange("p (kc t) -> p kc t", t=T)
        ptrv = ptr.ap().rearrange("p (b t) -> p b t", t=128)
        ptr_free = None
        x0T_done = None
        for tg in range(TG):
            for g8 in range(2):
                tok_tr = None
                for j in range(8):
                    dc = g8 * 8 + j
                    w = list(bulk_w)
                    if j == 0 and ptr_free is not None:
                        w.append(ptr_free)
                    tok_tr = pe(lambda t, o=ptr[:, j * 128:(j + 1) * 128],
                                i=bigv[:, tg, dc * 128:(dc + 1) * 128], idn=identb[:, :]:
                                t.transpose(o, i, idn), w, inc=(j == 7))
                dst = xTv[:, g8 * 8:(g8 + 1) * 8, tg * 128:(tg + 1) * 128]
                x0T_done = dve(lambda v, d=dst, s=ptrv: v.tensor_copy(d, s),
                               [(s_pe, tok_tr)], inc=True)
                ptr_free = (s_dve, x0T_done)
        x0T_w = [(s_dve, x0T_done)]
        tok_last_x0big_pe = s_pe.n

        if "x0t" in tap_d:
            sdma(tap_d["x0t"][:, :], xT[:, :], s_out, x0T_w)

        wslv = [w.ap().rearrange("p (kc m) -> p kc m", m=128) for w in wsl]
        slot_ready = [None] * 4
        slot_lastuse = [None] * 4
        nslice = H * 3

        def issue_slice_load(j):
            hh, m = divmod(j, 3)
            mat = (wq_d, wk_d, wv_d)[m]
            slot = j % 4
            free_w = [slot_lastuse[slot]] if slot_lastuse[slot] else []
            src = mat[:, hh * 128:(hh + 1) * 128]
            free_f = [(w[0], w[1], True) for w in free_w]
            gdma(wslv[slot][:, 0:8], src[0:1024, :].rearrange("(kc p) m -> p kc m", p=128),
                 s_wsl[slot], free_f)
            tok = gdma(wslv[slot][:, 8:16], src[1024:2048, :].rearrange("(kc p) m -> p kc m", p=128),
                       s_wsl[slot], free_f)
            slot_ready[slot] = (s_wsl[slot], tok)

        for j in range(min(4, nslice)):
            issue_slice_load(j)

        qkT_free = None
        vT_free = None
        probs_free = None
        probsT_free = None
        psav_free = None
        rope_prev = None
        outh_tok = None
        qT_ready = kT_ready = None

        maskF = maskt[:, 3 * 128:4 * 128]
        maskD = maskt[:, 11 * 128:12 * 128]
        probs_free = [None] * 8
        stat_free = [None] * 8
        pt_last_act = None
        pt_memset_tok = None

        for h in range(H):
            for m, (arena, scale) in enumerate(((qT, QSCALE), (kT, 1.0))):
                slot = (h * 3 + m) % 4
                rw = [slot_ready[slot]]
                tok_rope = None
                for tb in range(4):
                    bank = f"ps{tb % 2}"
                    pairs = [(wslv[slot][:, kc, :], xTv[:, kc, tb * 512:(tb + 1) * 512])
                             for kc in range(KC)]
                    tok_mm = mm_group(ps[tb % 2][:, :], pairs, rw + x0T_w, bank=bank)
                    slot_lastuse[slot] = (s_pe, tok_mm)
                    tcols = slice(tb * 512, (tb + 1) * 512)
                    q_raw = ropet[:, 0:512]
                    t_sw = ropet[:, 512:1024]
                    wv_ev = [(s_pe, tok_mm)]
                    if rope_prev is not None:
                        wv_ev.append(rope_prev)
                    tok_ev = act(lambda a, o=q_raw, i=ps[tb % 2][:, :], sc=scale:
                                 a.activation(o, i, AF.Copy, scale=sc), wv_ev, inc=True)
                    bank_free[bank] = [(s_act, tok_ev)]
                    dve(lambda v, o=t_sw[0:64, :], i=q_raw[64:128, :]: v.tensor_copy(o, i),
                        [(s_act, tok_ev)] + ([rope_prev] if rope_prev else []))
                    tok_sw = dve(lambda v, o=t_sw[64:128, :], i=q_raw[0:64, :]:
                                 v.tensor_copy(o, i), [], inc=True)
                    dve(lambda v, o=t_sw, i=sin2[:, tcols]:
                        v.tensor_tensor(o, o, i, ALU.mult), bulk_w)
                    dve(lambda v, o=q_raw, i=cos2[:, tcols]: v.tensor_tensor(o, o, i, ALU.mult),
                        [])
                    ws = [qkT_free] if qkT_free else []
                    tok_rope = dve(lambda v, o=arena[:, tcols], i0=t_sw, s=sgncol[:, :], i1=q_raw:
                                   v.scalar_tensor_tensor(o, i0, s, i1, ALU.mult, ALU.add),
                                   ws + bulk_w, inc=True)
                    rope_prev = (s_dve, tok_rope)
                if m == 0:
                    qT_ready = (s_dve, tok_rope)
                else:
                    kT_ready = (s_dve, tok_rope)
            slot = (h * 3 + 2) % 4
            rw = [slot_ready[slot]]
            vtok_ev = None
            for tb in range(4):
                bank = f"ps{tb % 2}"
                pairs = [(wslv[slot][:, kc, :], xTv[:, kc, tb * 512:(tb + 1) * 512])
                         for kc in range(KC)]
                tok_mm = mm_group(ps[tb % 2][:, :], pairs, rw + x0T_w, bank=bank)
                slot_lastuse[slot] = (s_pe, tok_mm)
                wv_ev = [(s_pe, tok_mm)]
                if vT_free is not None:
                    wv_ev.append(vT_free)
                vtok_ev = act(lambda a, o=vT[:, tb * 512:(tb + 1) * 512], i=ps[tb % 2][:, :]:
                              a.activation(o, i, AF.Copy), wv_ev, inc=True)
                bank_free[bank] = [(s_act, vtok_ev)]
            vT_feat = (s_act, vtok_ev)
            for jn in range(h * 3 + 3, min(h * 3 + 6, nslice)):
                issue_slice_load(jn)
            tok_vtok = None
            for g8 in range(2):
                tok_tr = None
                for jj in range(8):
                    i = g8 * 8 + jj
                    w = [vT_feat]
                    if jj == 0 and ptr_free is not None:
                        w.append(ptr_free)
                    tok_tr = pe(lambda t, o=ptr[:, jj * 128:(jj + 1) * 128],
                                i_=vT[:, i * 128:(i + 1) * 128], idn=identb[:, :]:
                                t.transpose(o, i_, idn), w, inc=(jj == 7))
                dst = vT.ap().rearrange("p (b t) -> p b t", t=128)[:, g8 * 8:(g8 + 1) * 8, :]
                tok_vtok = act(lambda a, d=dst, s=ptrv: a.activation(d, s, AF.Copy),
                               [(s_pe, tok_tr)], inc=True)
                ptr_free = (s_act, tok_vtok)
            v_tok_ready = (s_act, tok_vtok)

            SKEW = 6
            units = [(qt, qi) for qt in range(4) for qi in range(4)]
            uinfo = [None] * 16

            def p1(u):
                nonlocal qkT_free
                qt, qi = units[u]
                qg = qt * 4 + qi
                w0b = max(0, qg - 8)
                nwb = qg + 1 - w0b
                wcols = nwb * 128
                par = u % 2
                pspar = psAB[par]
                pb = u % 8
                prb = probs[:, pb * 1152: pb * 1152 + wcols]
                st = pb * 4
                bw = list(stat_free[pb] or [])
                for reg in range(3):
                    k = f"ps{reg * 2 + par}"
                    if k in bank_free:
                        bf = bank_free[k]
                        bw += bf if isinstance(bf, list) else [bf]
                off = 0
                first = True
                tok_s = None
                while off < wcols:
                    cw = min(512, wcols - off)
                    tok_s = S.op("tensor",
                                 (lambda t, o=pspar[:, off:off + cw],
                                  a=qT[:, qg * 128:(qg + 1) * 128],
                                  b=kT[:, w0b * 128 + off: w0b * 128 + off + cw]:
                                  t.matmul(o, a, b, start=True, stop=True)),
                                 ([qT_ready, kT_ready] + bw) if first else [], s_pe)
                    first = False
                    off += cw
                qkT_free = (s_pe, tok_s)
                if qg >= 8:
                    S.op("tensor", (lambda t, o=pspar[:, 0:128], a=identb[:, :], b=maskF:
                                    t.matmul(o, a, b, start=False, stop=True)), bulk_w, None)
                tok_s = S.op("tensor",
                             (lambda t, o=pspar[:, wcols - 128:wcols], a=identb[:, :], b=maskD:
                              t.matmul(o, a, b, start=False, stop=True)), bulk_w, s_pe)
                tok_max = dve(lambda v, o=stat[:, st:st + 1], i=pspar[:, 0:wcols]:
                              v.reduce_max(o, i, AX.X, negate=True), [(s_pe, tok_s)], inc=True)
                pw = [(s_dve, tok_max)]
                if probs_free[pb]:
                    pw.append(probs_free[pb])
                tok_exp = act(lambda a, o=prb, i=pspar[:, 0:wcols], b=stat[:, st:st + 1]:
                              a.activation(o, i, AF.Exp, bias=b), pw, inc=True)
                stat_free[pb] = [(s_act, tok_exp)]
                for reg in range(3):
                    bank_free[f"ps{reg * 2 + par}"] = [(s_act, tok_exp)]
                tok_rs = dve(lambda v, o=stat[:, st + 1:st + 2], i=prb:
                             v.reduce_sum(o, i, AX.X), [(s_act, tok_exp)], inc=True)
                tok_rc = dve(lambda v, o=stat[:, st + 2:st + 3], i=stat[:, st + 1:st + 2]:
                             v.reciprocal(o, i), [(s_dve, tok_rs)], inc=True)
                tok_nrm = dve(lambda v, o=prb, s=stat[:, st + 2:st + 3]:
                              v.tensor_scalar(o, o, s, None, ALU.mult),
                              [(s_dve, tok_rc)], inc=True)
                uinfo[u] = (qt, qi, qg, w0b, nwb, pb, tok_nrm)

            def p2(u):
                nonlocal ptr_free, pt_last_act, pt_memset_tok
                qt, qi, qg, w0b, nwb, pb, tok_nrm = uinfo[u]
                s0b = max(0, qt * 4 - 8)
                wb = qt * 4 + 4 - s0b
                prbase = pb * 1152
                pfw = [probsT_free] if probsT_free else []
                for p in range(wb):
                    blk = s0b + p
                    if blk < w0b or blk > qg:
                        pt_memset_tok = dve(
                            lambda v, o=probsT[:, p * 512 + qi * 128: p * 512 + (qi + 1) * 128]:
                            v.memset(o, 0.0), pfw, inc=True)
                for b0 in range(0, nwb, 8):
                    bn = min(8, nwb - b0)
                    tok_tr = None
                    for jj in range(bn):
                        w = [(s_dve, tok_nrm)]
                        if jj == 0 and ptr_free is not None:
                            w.append(ptr_free)
                        tok_tr = pe(lambda t, o=ptr[:, jj * 128:(jj + 1) * 128],
                                    i_=probs[:, prbase + (b0 + jj) * 128:
                                             prbase + (b0 + jj + 1) * 128],
                                    idn=identb[:, :]:
                                    t.transpose(o, i_, idn), w, inc=(jj == bn - 1))
                    p0 = w0b - s0b + b0
                    dst = probsT.ap().rearrange("p (b q) -> p b q", q=512)[
                        :, p0:p0 + bn, qi * 128:(qi + 1) * 128]
                    w2 = [(s_pe, tok_tr)] + pfw
                    tok_pt = act(lambda a, d=dst, s=ptrv[:, 0:bn, :]:
                                 a.activation(d, s, AF.Copy), w2, inc=True)
                    ptr_free = (s_act, tok_pt)
                    pt_last_act = tok_pt
                probs_free[pb] = (s_pe, tok_tr)

            def do_av(qt):
                nonlocal vT_free, probsT_free, psav_free, outh_tok
                s0b = max(0, qt * 4 - 8)
                wb = qt * 4 + 4 - s0b
                pairs = [(vT[:, (s0b + p) * 128:(s0b + p + 1) * 128],
                          probsT[:, p * 512:(p + 1) * 512]) for p in range(wb)]
                ew = [(s_act, pt_last_act), v_tok_ready]
                if pt_memset_tok is not None:
                    ew.append((s_dve, pt_memset_tok))
                if psav_free is not None:
                    ew.append(psav_free)
                tok_av = mm_group(psav[:, :], pairs, ew, bank=None)
                vT_free = (s_pe, tok_av)
                probsT_free = (s_pe, tok_av)
                w3 = [(s_pe, tok_av)]
                if h == 0 and qt == 0:
                    w3.append((s_pe, tok_last_x0big_pe))
                outh_tok = act(lambda a, o=big[:, h * T + qt * 512: h * T + (qt + 1) * 512],
                               i=psav[:, :]: a.activation(o, i, AF.Copy), w3, inc=True)
                psav_free = (s_act, outh_tok)

            for step in range(16 + SKEW):
                if step < 16:
                    p1(step)
                if step >= SKEW:
                    u = step - SKEW
                    p2(u)
                    if units[u][1] == 3:
                        do_av(units[u][0])
        outh_w = [(s_act, outh_tok)]

        if "outh" in tap_d:
            sdma(tap_d["outh"][:, :], big[:, :], s_out, outh_w)
        if "qt" in tap_d:
            sdma(tap_d["qt"][:, :], qT[:, :], s_out, outh_w)
            sdma(tap_d["kt"][:, :], kT[:, :], s_out, outh_w)
            sdma(tap_d["vt"][:, :], vT[:, :], s_out, outh_w)
        if "probs" in tap_d:
            sdma(tap_d["probs"][:, :], probs[:, :], s_out, outh_w)
            sdma(tap_d["probst"][:, :], probsT[:, :], s_out, outh_w)
            sdma(tap_d["stat"][:, :], stat[:, :], s_out, outh_w)

        gateA = [(s_pe, s_pe.n), (s_act, s_act.n), (s_dve, s_dve.n)]
        ctxH.close()

        ctxW = ExitStack()
        ctx.callback(ctxW.close)
        woslab = [ctxW.enter_context(nc.sbuf_tensor(f"woslab{i}", [128, KC * 256], BF16))
                  for i in range(2)]
        x0sl = [ctxW.enter_context(nc.sbuf_tensor(f"x0sl{i}", [128, 512], F32))
                for i in range(4)]
        ysl = ctxW.enter_context(nc.sbuf_tensor("ysl", [128, 4 * 512], F32))
        yrow = [ctxW.enter_context(nc.sbuf_tensor(f"yrow{i}", [128, D], F32))
                for i in range(2)]
        x1fr = ctxW.enter_context(nc.sbuf_tensor("x1fr", [128, 2 * D], F32))
        x1bf = ctxW.enter_context(nc.sbuf_tensor("x1bf", [128, 2 * D], BF16))

        woslabv = [w_.ap().rearrange("p (hc d) -> p hc d", d=256) for w_ in woslab]
        wo_lastuse = [None, None]
        x0_lastuse = [None] * 4
        ysl_lastuse = [None] * 4
        wo_tok = [None, None]

        def load_woslab(dt):
            slot = dt % 2
            w = list(gateA)
            if wo_lastuse[slot]:
                w.append((wo_lastuse[slot][0], wo_lastuse[slot][1], True))
            src = wo_d[:, dt * 256:(dt + 1) * 256]
            gdma(woslabv[slot][:, 0:8], src[0:1024, :].rearrange("(hc p) d -> p hc d", p=128),
                 s_wo, w)
            wo_tok[slot] = gdma(woslabv[slot][:, 8:16],
                                src[1024:2048, :].rearrange("(hc p) d -> p hc d", p=128), s_wo, w)

        load_woslab(0)
        load_woslab(1)
        for dt in range(8):
            slot = dt % 2
            wo_w = [(s_wo, wo_tok[slot])]
            for tg in range(TG):
                sl = tg % 4
                w = list(gateA) + ([x0_lastuse[sl]] if x0_lastuse[sl] else [])
                tok_x0 = sdma(x0sl[sl][:, 0:256],
                              x0_d[tg * 128:(tg + 1) * 128, dt * 256:(dt + 1) * 256],
                              s_x0s[sl], w)
                bank = f"ps{tg % 2}"
                pairs = [(big[:, hc * T + tg * 128: hc * T + tg * 128 + 128],
                          woslabv[slot][:, hc, :]) for hc in range(H)]
                tok_mm = mm_group(ps[tg % 2][:, 0:256], pairs, outh_w + wo_w, bank=bank)
                wo_lastuse[slot] = (s_pe, tok_mm)
                w2 = [(s_pe, tok_mm), (s_x0s[sl], tok_x0)]
                if ysl_lastuse[sl]:
                    w2.append(ysl_lastuse[sl])
                tok_y = dve(lambda v, o=ysl[:, sl * 512:sl * 512 + 256], a=ps[tg % 2][:, 0:256],
                            b=x0sl[sl][:, 0:256]:
                            v.tensor_tensor(o, a, b, ALU.add), w2, inc=True)
                bank_free[bank] = (s_dve, tok_y)
                x0_lastuse[sl] = (s_dve, tok_y)
                tok_st = sdma(ybuf_d[tg * 128:(tg + 1) * 128, dt * 256:(dt + 1) * 256],
                              ysl[:, sl * 512:sl * 512 + 256], s_yw, [(s_dve, tok_y)])
                ysl_lastuse[sl] = (s_yw, tok_st)
            if dt + 2 < 8:
                load_woslab(dt + 2)
        yw_all = [(s_yw, s_yw.n)]

        if "y" in tap_d:
            for tg in range(TG):
                sdma(tap_d["y"][tg * 128:(tg + 1) * 128, :], ybuf_d[tg * 128:(tg + 1) * 128, :],
                     s_out, yw_all)

        yr_lastuse = [None, None]
        x1fr_lastuse = [None, None]
        x1bf_lastuse = [None, None]
        x1T_evac = []
        c0 = 40
        for tg in range(TG):
            sl = tg % 2
            yr = yrow[sl][:, :]
            jnk = x1fr[:, sl * D:(sl + 1) * D]
            w = list(yw_all)
            if yr_lastuse[sl]:
                w += yr_lastuse[sl]
            tok_ld = sdma(yr, ybuf_d[tg * 128:(tg + 1) * 128, :], s_yr[sl], w)
            ld_w = [(s_yr[sl], tok_ld)]
            tok_s1 = dve(lambda v, o=stat[:, c0:c0 + 1], i=yr: v.reduce_sum(o, i, AX.X),
                         ld_w, inc=True)
            tok_ssq = dve(lambda v, o=jnk, a=yr: v.tensor_tensor(o, a, a, ALU.mult),
                          ld_w + ([x1fr_lastuse[sl]] if x1fr_lastuse[sl] else []), inc=True)
            tok_s2 = dve(lambda v, o=stat[:, c0 + 1:c0 + 2], i=jnk: v.reduce_sum(o, i, AX.X),
                         [(s_dve, tok_ssq)], inc=True)
            tok_m = dve(lambda v, o=stat[:, c0 + 2:c0 + 3], i=stat[:, c0:c0 + 1]:
                        v.tensor_scalar(o, i, 1.0 / D, None, ALU.mult),
                        [(s_dve, tok_s1)], inc=True)
            tok_e = dve(lambda v, o=stat[:, c0 + 3:c0 + 4], i=stat[:, c0 + 1:c0 + 2]:
                        v.tensor_scalar(o, i, 1.0 / D, None, ALU.mult),
                        [(s_dve, tok_s2)], inc=True)
            tok_mq = dve(lambda v, o=stat[:, c0 + 4:c0 + 5], a=stat[:, c0 + 2:c0 + 3]:
                         v.tensor_tensor(o, a, a, ALU.mult), [(s_dve, tok_m)], inc=True)
            tok_var = dve(lambda v, o=stat[:, c0 + 3:c0 + 4], a=stat[:, c0 + 3:c0 + 4],
                          b=stat[:, c0 + 4:c0 + 5]: v.tensor_tensor(o, a, b, ALU.subtract),
                          [(s_dve, tok_e), (s_dve, tok_mq)], inc=True)
            act(lambda a, o=stat[:, c0 + 4:c0 + 5], i=stat[:, c0 + 3:c0 + 4],
                b=kcol[:, 1:2]: a.activation(o, i, AF.Ln, bias=b),
                [(s_dve, tok_var)] + bulk_w)
            tok_rstd = act(lambda a, o=stat[:, c0 + 5:c0 + 6], i=stat[:, c0 + 4:c0 + 5]:
                           a.activation(o, i, AF.Exp, scale=-0.5), [], inc=True)
            tok_nmr = dve(lambda v, o=stat[:, c0 + 6:c0 + 7], i=stat[:, c0 + 2:c0 + 3],
                          r=stat[:, c0 + 5:c0 + 6]:
                          v.scalar_tensor_tensor(o, i, -1.0, r, ALU.mult, ALU.mult),
                          [(s_act, tok_rstd), (s_dve, tok_m)], inc=True)
            tok_x1f = dve(lambda v, o=jnk, i=yr, r=stat[:, c0 + 5:c0 + 6], n=stat[:, c0 + 6:c0 + 7]:
                          v.tensor_scalar(o, i, r, n, ALU.mult, ALU.add),
                          [(s_dve, tok_ssq), (s_dve, tok_nmr)], inc=True)
            wbf = [(s_dve, tok_nmr)] + ([x1bf_lastuse[sl]] if x1bf_lastuse[sl] else [])
            tok_x1b = act(lambda a, o=x1bf[:, sl * D:(sl + 1) * D], i=yr,
                          s=stat[:, c0 + 5:c0 + 6], b=stat[:, c0 + 6:c0 + 7]:
                          a.activation(o, i, AF.Identity, scale=s, bias=b), wbf, inc=True)
            yr_lastuse[sl] = [(s_act, tok_x1b), (s_dve, tok_x1f)]
            tok_w = sdma(x1f_d[tg * 128:(tg + 1) * 128, :], jnk, s_x1w, [(s_dve, tok_x1f)])
            x1fr_lastuse[sl] = (s_x1w, tok_w)
            tok_tr = None
            for g8 in range(2):
                for jj in range(8):
                    dc = g8 * 8 + jj
                    w = [(s_act, tok_x1b)]
                    if jj == 0 and ptr_free is not None:
                        w.append(ptr_free)
                    tok_tr = pe(lambda t, o=ptr[:, jj * 128:(jj + 1) * 128],
                                i_=x1bf[:, sl * D + dc * 128: sl * D + (dc + 1) * 128],
                                idn=identb[:, :]: t.transpose(o, i_, idn), w, inc=(jj == 7))
                dst = xTv[:, g8 * 8:(g8 + 1) * 8, tg * 128:(tg + 1) * 128]
                tok_ev = dve(lambda v, d=dst, s=ptrv: v.tensor_copy(d, s),
                             [(s_pe, tok_tr)], inc=True)
                ptr_free = (s_dve, tok_ev)
                x1T_evac.append(tok_ev)
            x1bf_lastuse[sl] = (s_pe, tok_tr)
        x1T_w = [(s_dve, max(x1T_evac))]
        x1w_all = [(s_x1w, s_x1w.n)]

        if "x1t" in tap_d:
            sdma(tap_d["x1t"][:, :], xT[:, :], s_out, x1T_w)
        if "x1" in tap_d:
            for tg in range(TG):
                sdma(tap_d["x1"][tg * 128:(tg + 1) * 128, :], x1f_d[tg * 128:(tg + 1) * 128, :],
                     s_out, x1w_all)

        if upto == "ln1":
            S.op("sync", lambda s, n=s_out.n: s.wait_ge(s_out.h, n), x1w_all + yw_all, None)
            S.check()
            _emit_block(nc, S)
            return nc

        gateL = [(s_pe, s_pe.n), (s_act, s_act.n), (s_dve, s_dve.n)] + list(x1w_all)
        gateB = [(s_pe, s_pe.n), (s_act, s_act.n), (s_dve, s_dve.n)] + list(yw_all) + list(x1w_all)
        wrbv = wrb.ap().rearrange("p (kc e) -> p kc e", e=E)
        tok_rmm = None
        for tg in range(TG):
            pairs = [(xTv[:, kc, tg * 128:(tg + 1) * 128], wrbv[:, kc, :]) for kc in range(KC)]
            ew = x1T_w + bulk_w
            if tg == 0:
                ew = ew + gateB
                if "ps0" in bank_free:
                    ew.append(bank_free["ps0"])
            tok_rmm = mm_group(ps[0][:, tg * 8:(tg + 1) * 8], pairs, ew, bank=None)
        tok_rt = dve(lambda v, o=stat[:, 32:33], i=tempcol[:, :]: v.reciprocal(o, i),
                     bulk_w, inc=True)
        tok_inh = act(lambda a, o=rtr[:, 0:128], i=ps[0][:, 0:128], s=stat[:, 32:33]:
                      a.activation(o, i, AF.Sigmoid, scale=s),
                      [(s_pe, tok_rmm), (s_dve, tok_rt)] + gateB, inc=True)
        bank_free["ps0"] = (s_act, tok_inh)
        dist = ysl[:, 0:128]
        wz = ysl[:, 128:256]
        fbv = ysl[:, 256:384]
        wv_ = ysl[:, 384:512]
        fb2 = ysl[:, 512:640]
        fbw = ysl[:, 640:768]
        t1v = ysl[:, 768:896]
        wsc = gateB + list(yw_all)
        tok_dist = act(lambda a, o=dist, i=rtr[:, 0:128], b=kcol[:, 0:1]:
                       a.activation(o, i, AF.Abs, bias=b),
                       [(s_act, tok_inh)] + wsc + bulk_w, inc=True)
        act(lambda a, o=wz, i=dist: a.activation(o, i, AF.Exp, scale=-10.0), [])
        tok_fb = act(lambda a, o=fbv, i=dist: a.activation(o, i, AF.Exp, scale=float(-1.0 / 0.3)),
                     [], inc=True)
        dve(lambda v, o=t1v, i=rtr[:, 0:128]: v.tensor_scalar(o, i, GOLDEN_LOWER, None, ALU.is_ge),
            [(s_act, tok_fb)] + wsc)
        dve(lambda v, o=wv_, a=wz, b=t1v: v.tensor_tensor(o, a, b, ALU.mult), [])
        dve(lambda v, o=t1v, i=rtr[:, 0:128]: v.tensor_scalar(o, i, GOLDEN_UPPER, None, ALU.is_le), [])
        tok_w = dve(lambda v, o=wv_, a=wv_, b=t1v: v.tensor_tensor(o, a, b, ALU.mult), [], inc=True)
        wtok_toks = []
        r0 = 48
        for tg in range(TG):
            s8 = slice(tg * 8, (tg + 1) * 8)
            t = dve(lambda v, o=stat[:, r0:r0 + 1], i=wv_[:, s8]: v.reduce_sum(o, i, AX.X),
                    [(s_dve, tok_w)], inc=True)
            t = dve(lambda v, o=stat[:, r0:r0 + 1], i=stat[:, r0:r0 + 1]:
                    v.tensor_scalar(o, i, 1e-8, None, ALU.is_lt), [(s_dve, t)], inc=True)
            tok_ne = t
            t = dve(lambda v, o=stat[:, r0 + 1:r0 + 2], i=fbv[:, s8]: v.reduce_max(o, i, AX.X),
                    [], inc=True)
            t = dve(lambda v, o=fb2[:, s8], i=fbv[:, s8], m=stat[:, r0 + 1:r0 + 2]:
                    v.tensor_scalar(o, i, m, None, ALU.is_ge), [(s_dve, t)], inc=True)
            t = dve(lambda v, o=fb2[:, s8], i0=fb2[:, s8], i1=fbv[:, s8]:
                    v.scalar_tensor_tensor(o, i0, -2.0, i1, ALU.mult, ALU.add),
                    [(s_dve, t)], inc=True)
            t = dve(lambda v, o=stat[:, r0 + 1:r0 + 2], i=fb2[:, s8]: v.reduce_max(o, i, AX.X),
                    [(s_dve, t)], inc=True)
            t = dve(lambda v, o=fb2[:, s8], i=fbv[:, s8], m=stat[:, r0 + 1:r0 + 2]:
                    v.tensor_scalar(o, i, m, None, ALU.is_ge), [(s_dve, t)], inc=True)
            t = dve(lambda v, o=fbw[:, s8], a=fbv[:, s8], b=fb2[:, s8]:
                    v.tensor_tensor(o, a, b, ALU.mult), [(s_dve, t)], inc=True)
            t = dve(lambda v, o=stat[:, r0 + 1:r0 + 2], i=fbw[:, s8]: v.reduce_sum(o, i, AX.X),
                    [(s_dve, t)], inc=True)
            t = dve(lambda v, o=stat[:, r0 + 1:r0 + 2], i=stat[:, r0 + 1:r0 + 2]:
                    v.tensor_scalar(o, i, 1e-8, None, ALU.max), [(s_dve, t)], inc=True)
            t = dve(lambda v, o=stat[:, r0 + 2:r0 + 3], i=stat[:, r0 + 1:r0 + 2]:
                    v.reciprocal(o, i), [(s_dve, t)], inc=True)
            t = dve(lambda v, o=fbw[:, s8], i=fbw[:, s8], m=stat[:, r0 + 2:r0 + 3]:
                    v.tensor_scalar(o, i, m, None, ALU.mult), [(s_dve, t)], inc=True)
            t = dve(lambda v, o=t1v[:, s8], a=fbw[:, s8], b=wv_[:, s8]:
                    v.tensor_tensor(o, a, b, ALU.subtract), [(s_dve, t)], inc=True)
            t = dve(lambda v, o=t1v[:, s8], i0=t1v[:, s8], m=stat[:, r0:r0 + 1], i1=wv_[:, s8]:
                    v.scalar_tensor_tensor(o, i0, m, i1, ALU.mult, ALU.add),
                    [(s_dve, t), (s_dve, tok_ne)], inc=True)
            t = dve(lambda v, o=stat[:, r0 + 1:r0 + 2], i=t1v[:, s8]: v.reduce_sum(o, i, AX.X),
                    [(s_dve, t)], inc=True)
            t = dve(lambda v, o=stat[:, r0 + 1:r0 + 2], i=stat[:, r0 + 1:r0 + 2]:
                    v.tensor_scalar(o, i, 1e-8, None, ALU.max), [(s_dve, t)], inc=True)
            t = dve(lambda v, o=stat[:, r0 + 2:r0 + 3], i=stat[:, r0 + 1:r0 + 2]:
                    v.reciprocal(o, i), [(s_dve, t)], inc=True)
            tok_wt = dve(lambda v, o=rtr[:, 160 + tg * 8:160 + (tg + 1) * 8], i=t1v[:, s8],
                         m=stat[:, r0 + 2:r0 + 3]: v.tensor_scalar(o, i, m, None, ALU.mult),
                         [(s_dve, t)], inc=True)
            wtok_toks.append(tok_wt)
        wtok_w = [(s_dve, max(wtok_toks))]

        if "router" in tap_d:
            sdma(tap_d["router"][:, :], rtr[:, 160:288], s_out, wtok_w)

        if upto == "router":
            S.op("sync", lambda s, n=s_out.n: s.wait_ge(s_out.h, n),
                 x1w_all + yw_all + wtok_w, None)
            S.check()
            _emit_block(nc, S)
            return nc

        ctxW.close()
        ctxA.close()
        ctxM = ExitStack()
        ctx.callback(ctxM.close)
        wgu_t = [ctxM.enter_context(nc.sbuf_tensor(f"wgu{i}", [128, 2 * 16 * 128], BF16))
                 for i in range(2)]
        wd_t = [ctxM.enter_context(nc.sbuf_tensor(f"wdt{i}", [128, 8 * DT], BF16))
                for i in range(2)]
        h_t = ctxM.enter_context(nc.sbuf_tensor("h_t", [128, 8 * TT], BF16))
        x1row = ctxM.enter_context(nc.sbuf_tensor("x1row", [128, D], F32))
        y2x = ctxM.enter_context(nc.sbuf_tensor("y2x", [128, 2 * D], F32))
        out_acc = ctxM.enter_context(nc.sbuf_tensor("out_acc", [128, 8 * D], F32))

        gateC = list(gateL)
        accgate = list(gateL) + list(yw_all) + wtok_w

        def h_cols(fc_local, t0, tn):
            return h_t[:, fc_local * TT + t0: fc_local * TT + t0 + tn]

        def wgu_tile(slot, mat, kc):
            b = mat * 2048 + kc * 128
            return wgu_t[slot][:, b:b + 128]

        def wgu_dst(slot, mat, kc0):
            b = mat * 2048 + kc0 * 128
            return wgu_t[slot].ap()[:, b:b + 8 * 128].rearrange("p (kc f) -> p kc f", f=128)

        def wd_tile(slot, fc_local):
            return wd_t[slot][:, fc_local * DT: (fc_local + 1) * DT]

        def wd_dst(slot):
            return wd_t[slot].ap()[:, :].rearrange("p (fc d) -> p fc d", d=DT)

        gu_group = 0
        d_group = 0
        wgwu_lastuse = [None, None]
        wd_lastuse = [None, None]
        h_lastuse = None
        x1r_lastuse = None
        acc_started = False

        def load_wgwu(e, half, sidx, slot):
            f0 = half * 1024 + sidx * 128
            w = list(gateC)
            if wgwu_lastuse[slot]:
                w.append((wgwu_lastuse[slot][0], wgwu_lastuse[slot][1], True))
            tok = None
            for mat, md in ((0, wg_d), (1, wu_d)):
                src = md[e * D:(e + 1) * D, f0:f0 + 128]
                gdma(wgu_dst(slot, mat, 0), src[0:1024, :].rearrange("(kc p) f -> p kc f", p=128),
                     s_wgwu[slot], w)
                tok = gdma(wgu_dst(slot, mat, 8),
                           src[1024:2048, :].rearrange("(kc p) f -> p kc f", p=128),
                           s_wgwu[slot], w)
            return tok

        ln2_tails = []
        ln2_state = {"x1_free": None, "y2_free": [None, None]}

        def make_ln2_closures(tile):
            def tail(tg8, tg):
                y2p = y2x[:, (tg8 % 2) * D:(tg8 % 2 + 1) * D]
                tok_s1 = dve(lambda v, o=stat[:, c0:c0 + 1], i=y2p:
                             v.reduce_sum(o, i, AX.X), [], inc=True)
                tok_ssq = dve(lambda v, o=x1row[:, :], a=y2p:
                              v.tensor_tensor(o, a, a, ALU.mult),
                              [ln2_state["x1_free"]], inc=True)
                tok_s2 = dve(lambda v, o=stat[:, c0 + 1:c0 + 2], i=x1row[:, :]:
                             v.reduce_sum(o, i, AX.X), [(s_dve, tok_ssq)], inc=True)
                tok_m = dve(lambda v, o=stat[:, c0 + 2:c0 + 3], i=stat[:, c0:c0 + 1]:
                            v.tensor_scalar(o, i, 1.0 / D, None, ALU.mult),
                            [(s_dve, tok_s1)], inc=True)
                tok_e = dve(lambda v, o=stat[:, c0 + 3:c0 + 4], i=stat[:, c0 + 1:c0 + 2]:
                            v.tensor_scalar(o, i, 1.0 / D, None, ALU.mult),
                            [(s_dve, tok_s2)], inc=True)
                tok_mq = dve(lambda v, o=stat[:, c0 + 4:c0 + 5], a=stat[:, c0 + 2:c0 + 3]:
                             v.tensor_tensor(o, a, a, ALU.mult), [(s_dve, tok_m)], inc=True)
                tok_var = dve(lambda v, o=stat[:, c0 + 3:c0 + 4], a=stat[:, c0 + 3:c0 + 4],
                              b=stat[:, c0 + 4:c0 + 5]: v.tensor_tensor(o, a, b, ALU.subtract),
                              [(s_dve, tok_e), (s_dve, tok_mq)], inc=True)
                act(lambda a, o=stat[:, c0 + 4:c0 + 5], i=stat[:, c0 + 3:c0 + 4],
                    b=kcol[:, 1:2]: a.activation(o, i, AF.Ln, bias=b), [(s_dve, tok_var)])
                tok_rstd = act(lambda a, o=stat[:, c0 + 5:c0 + 6], i=stat[:, c0 + 4:c0 + 5]:
                               a.activation(o, i, AF.Exp, scale=-0.5), [], inc=True)
                tok_nmr = dve(lambda v, o=stat[:, c0 + 6:c0 + 7], i=stat[:, c0 + 2:c0 + 3],
                              r=stat[:, c0 + 5:c0 + 6]:
                              v.scalar_tensor_tensor(o, i, -1.0, r, ALU.mult, ALU.mult),
                              [(s_act, tok_rstd), (s_dve, tok_m)], inc=True)
                tok_o = dve(lambda v, o=x1row[:, :], i=y2p, r=stat[:, c0 + 5:c0 + 6],
                            n=stat[:, c0 + 6:c0 + 7]:
                            v.tensor_scalar(o, i, r, n, ALU.mult, ALU.add),
                            [(s_dve, tok_nmr)], inc=True)
                tok_st = sdma(out_d[tg * 128:(tg + 1) * 128, :], x1row[:, :], s_out,
                              [(s_dve, tok_o)])
                ln2_state["x1_free"] = (s_out, tok_st)
                ln2_state["y2_free"][tg8 % 2] = (s_dve, tok_o)

            def load_add(tg8, tg):
                acc_row = out_acc[:, tg8 * D:(tg8 + 1) * D]
                y2p = y2x[:, (tg8 % 2) * D:(tg8 % 2 + 1) * D]
                w = list(x1w_all) + list(gateL)
                if ln2_state["x1_free"]:
                    w.append(ln2_state["x1_free"])
                tok_ld = sdma(x1row[:, :], x1f_d[tg * 128:(tg + 1) * 128, :], s_x1r, w)
                yfree = ln2_state["y2_free"][tg8 % 2]
                dve(lambda v, o=y2p, a=acc_row, b=x1row[:, :]:
                    v.tensor_tensor(o, a, b, ALU.add),
                    [(s_x1r, tok_ld)] + ([yfree] if yfree else []), inc=True)

            def closure(k):
                def run():
                    if k >= 1:
                        tail(k - 1, tile * 8 + k - 1)
                    if k < 8:
                        load_add(k, tile * 8 + k)
                return run
            return [closure(k) for k in range(9)]

        for tile in range(NTILE):
            t0 = tile * TT
            for e in range(E):
                for half in range(NHALF):
                    slab_tok = [load_wgwu(e, half, 0, 0), load_wgwu(e, half, 1, 1)]
                    next_sidx = 2
                    tok_h = None
                    for sidx in range(8):
                        slot = sidx % 2
                        sw = [(s_wgwu[slot], slab_tok[slot])]
                        fc_local = sidx
                        for ts in range(2):
                            gp = gu_group % 2
                            gu_group += 1
                            rhs = [xTv[:, kc, t0 + ts * 512: t0 + (ts + 1) * 512]
                                   for kc in range(KC)]
                            gpairs = [(wgu_tile(slot, 0, kc), rhs[kc]) for kc in range(KC)]
                            upairs = [(wgu_tile(slot, 1, kc), rhs[kc]) for kc in range(KC)]
                            ew = sw + x1T_w
                            if tile == 0 and e == 0 and half == 0:
                                ew = ew + gateC
                            tok_g = mm_group(ps[gp][:, :], gpairs, ew, bank=f"ps{gp}")
                            tok_u = mm_group(ps[2 + gp][:, :], upairs, [], bank=f"ps{2 + gp}")
                            wgwu_lastuse[slot] = (s_pe, tok_u)
                            hsl = h_cols(fc_local, ts * 512, 512)
                            hw = [h_lastuse] if h_lastuse else list(gateL)
                            tok_si = act(lambda a, o=hsl, i=ps[gp][:, :]:
                                         a.activation(o, i, AF.Silu),
                                         [(s_pe, tok_g)] + hw, inc=True)
                            bank_free[f"ps{gp}"] = (s_act, tok_si)
                            tok_h = dve(lambda v, o=hsl, i=ps[2 + gp][:, :]:
                                        v.tensor_tensor(o, o, i, ALU.mult),
                                        [(s_pe, tok_u), (s_act, tok_si)], inc=True)
                            bank_free[f"ps{2 + gp}"] = (s_dve, tok_h)
                        if next_sidx < 8:
                            slab_tok[next_sidx % 2] = load_wgwu(e, half, next_sidx, next_sidx % 2)
                            next_sidx += 1
                        if ln2_tails:
                            ln2_tails.pop(0)()
                    h_ready = (s_dve, tok_h)
                    tok_d = None
                    for dt in range(4):
                        slot = dt % 2
                        w = list(gateC) + ([wd_lastuse[slot]] if wd_lastuse[slot] else [])
                        src = wd_d[e * D + half * 1024: e * D + (half + 1) * 1024,
                                   dt * 512:(dt + 1) * 512]
                        tok_wd = gdma(wd_dst(slot), src.rearrange("(fc p) d -> p fc d", p=128),
                                      s_wd[slot], w)
                        for tg8 in range(8):
                            dp = d_group % 2
                            d_group += 1
                            pairs = [(h_cols(fc, tg8 * 128, 128), wd_tile(slot, fc))
                                     for fc in range(8)]
                            tok_d = mm_group(ps[4 + dp][:, :], pairs,
                                             [h_ready, (s_wd[slot], tok_wd)], bank=f"ps{4 + dp}")
                            wd_lastuse[slot] = (s_pe, tok_d)
                            acc_sl = out_acc[:, tg8 * D + dt * 512: tg8 * D + (dt + 1) * 512]
                            wcol = rtr[:, 160 + (tile * 8 + tg8) * 8 + e:
                                       161 + (tile * 8 + tg8) * 8 + e]
                            ew2 = [(s_pe, tok_d)] + wtok_w
                            if not acc_started:
                                ew2 = ew2 + accgate
                            if e == 0 and half == 0:
                                tok_ac = dve(lambda v, o=acc_sl, i=ps[4 + dp][:, :], m=wcol:
                                             v.tensor_scalar(o, i, m, None, ALU.mult),
                                             ew2, inc=True)
                            else:
                                tok_ac = dve(lambda v, o=acc_sl, i=ps[4 + dp][:, :], m=wcol:
                                             v.scalar_tensor_tensor(o, i, m, o, ALU.mult, ALU.add),
                                             ew2, inc=True)
                            acc_started = True
                            bank_free[f"ps{4 + dp}"] = (s_dve, tok_ac)
                    h_lastuse = (s_pe, tok_d)
            ln2_tails.extend(make_ln2_closures(tile))
            if tile == 99:
                pass
            while ln2_tails:
                ln2_tails.pop(0)()

        while ln2_tails:
            ln2_tails.pop(0)()

        S.op("sync", lambda s, n=s_out.n: s.wait_ge(s_out.h, n), (), None)
        S.check()
        _emit_block(nc, S)
    return nc


def _emit_block(nc, S):
    with nc.Block() as block:
        @block.tensor
        def _(t):
            S.replay("tensor", t)

        @block.vector
        def _(v):
            S.replay("vector", v)

        @block.scalar
        def _(sc):
            S.replay("scalar", sc)

        @block.gpsimd
        def _(g):
            S.replay("gpsimd", g)

        @block.sync
        def _(sy):
            S.replay("sync", sy)


def kernel(L0_attn_hidden_states, L0_attn_wq, L0_attn_wk, L0_attn_wv, L0_attn_wo,
           L0_ffn_Wg, L0_ffn_Wu, L0_ffn_Wd, L0_ffn_Wr, L0_ffn_temperature):
    x = np.ascontiguousarray(np.asarray(L0_attn_hidden_states, dtype=np.float32))
    base = {
        "wq": np.ascontiguousarray(np.asarray(L0_attn_wq, dtype=np.float32)),
        "wk": np.ascontiguousarray(np.asarray(L0_attn_wk, dtype=np.float32)),
        "wv": np.ascontiguousarray(np.asarray(L0_attn_wv, dtype=np.float32)),
        "wo": np.ascontiguousarray(np.asarray(L0_attn_wo, dtype=np.float32)),
        "wg3": np.ascontiguousarray(np.asarray(L0_ffn_Wg, dtype=np.float32).reshape(E * D, D)),
        "wu3": np.ascontiguousarray(np.asarray(L0_ffn_Wu, dtype=np.float32).reshape(E * D, D)),
        "wd3": np.ascontiguousarray(np.asarray(L0_ffn_Wd, dtype=np.float32).reshape(E * D, D)),
        "wr": np.ascontiguousarray(np.asarray(L0_ffn_Wr, dtype=np.float32)),
        "tempcol": np.broadcast_to(
            np.asarray(L0_ffn_temperature, dtype=np.float32).reshape(1, 1), (128, 1)).copy(),
    }
    cos2, sin2, sgn, ident, maskt = _host_consts()
    kcol = np.zeros((128, 4), np.float32)
    kcol[:, 0] = -GOLDEN_CENTER
    kcol[:, 1] = LN_EPS
    base.update({"cos2": np.asarray(cos2), "sin2": np.asarray(sin2), "sgncol": sgn,
                 "identb": np.asarray(ident), "maskt": np.asarray(maskt), "kcol": kcol})
    nc = build_nc()
    in_maps = [dict(base, x0=np.ascontiguousarray(x[b])) for b in range(NCORE)]
    res = run_bass_kernel_spmd(nc, in_maps, core_ids=list(range(NCORE)))
    return np.stack([res.results[b]["out"] for b in range(NCORE)], axis=0)


# version 18
# speedup vs baseline: 1.0356x; 1.0356x over previous
import numpy as np
import ml_dtypes
from contextlib import ExitStack

import concourse.bass as bass
import concourse.mybir as mybir
from concourse.bass_utils import run_bass_kernel_spmd

F32 = mybir.dt.float32
BF16 = mybir.dt.bfloat16
AF = mybir.ActivationFunctionType
ALU = mybir.AluOpType
AX = mybir.AxisListType

D = 2048
T = 2048
KC = 16
H = 16
HD = 128
E = 8
TG = 16
NCORE = 8
NEG = -1.0e30
QSCALE = float(np.float32(HD ** -0.5))
GOLDEN_CENTER = float(np.float32(1.0 / np.e))
GOLDEN_LOWER = float(np.float32(0.5 - np.log(4.0 / 3.0)))
GOLDEN_UPPER = 0.5
LN_EPS = 1e-5

NTILE = 2
TT = 1024
NHALF = 2
FT = 256
DT = 512


def _host_consts():
    inv_freq = (1.0 / (10000.0 ** (np.arange(0, HD, 2, dtype=np.float64) / HD))).astype(np.float32)
    pos = np.arange(T, dtype=np.float32)
    freqs = pos[None, :] * inv_freq[:, None]
    cos = np.cos(freqs).astype(np.float32)
    sin = np.sin(freqs).astype(np.float32)
    cos2 = np.concatenate([cos, cos], 0).astype(ml_dtypes.bfloat16)
    sin2 = np.concatenate([sin, sin], 0).astype(ml_dtypes.bfloat16)
    sgn = np.concatenate([np.full((64, 1), -1.0, np.float32),
                          np.full((64, 1), 1.0, np.float32)], 0)
    ident = np.eye(128, dtype=np.float32).astype(ml_dtypes.bfloat16)
    r = np.arange(128)[:, None]
    c = np.arange(128)[None, :]
    blocks = []
    for u in range(15):
        o = 11 - u
        if o == 8:
            m = np.where(c > r, 0.0, NEG)
        elif o == 0:
            m = np.where(c <= r, 0.0, NEG)
        elif 1 <= o <= 7:
            m = np.zeros((128, 128))
        else:
            m = np.full((128, 128), NEG)
        blocks.append(m)
    maskt = np.concatenate(blocks, 1).astype(np.float32).astype(ml_dtypes.bfloat16)
    return cos2, sin2, sgn, ident, maskt


class Sem:
    __slots__ = ("name", "n", "h", "max_waited")

    def __init__(self, name):
        self.name = name
        self.n = 0
        self.h = None
        self.max_waited = 0


class Sched:
    ENGINES = ("tensor", "vector", "scalar", "gpsimd", "sync")

    def __init__(self):
        self.q = {e: [] for e in self.ENGINES}
        self._lastw = {e: {} for e in self.ENGINES}
        self.sems = {}

    def sem(self, name):
        s = Sem(name)
        self.sems[name] = s
        return s

    def op(self, eng, fn, waits=(), inc=None, inck=1):
        wl = []
        for item in waits:
            if item is None:
                continue
            if len(item) == 3:
                s, tok, force = item
            else:
                s, tok = item
                force = False
            if s is None or tok is None or tok <= 0:
                continue
            lw = self._lastw[eng].get(s.name, 0)
            if force or tok > lw:
                tok2 = max(tok, lw)
                wl.append((s, tok2))
                self._lastw[eng][s.name] = tok2
                s.max_waited = max(s.max_waited, tok2)
        tok_out = None
        if inc is not None:
            inc.n += inck
            tok_out = inc.n
            self.q[eng].append((fn, wl, (inc, inck)))
        else:
            self.q[eng].append((fn, wl, None))
        return tok_out

    def check(self):
        for s in self.sems.values():
            assert s.max_waited <= s.n, f"sem {s.name}: waits {s.max_waited} > incs {s.n} (hang)"

    def replay(self, eng_name, eng):
        for fn, wl, inc in self.q[eng_name]:
            for s, tok in wl:
                eng.wait_ge(s.h, tok)
            instr = fn(eng)
            if inc is not None:
                s, k = inc
                instr.then_inc(s.h, k)


def build_nc(taps=(), upto="all"):
    nc = bass.Bass("TRN2", target_bir_lowering=False)
    S = Sched()

    x0_d = nc.dram_tensor("x0", [T, D], F32, kind="ExternalInput").ap()
    wq_d = nc.dram_tensor("wq", [D, D], F32, kind="ExternalInput").ap()
    wk_d = nc.dram_tensor("wk", [D, D], F32, kind="ExternalInput").ap()
    wv_d = nc.dram_tensor("wv", [D, D], F32, kind="ExternalInput").ap()
    wo_d = nc.dram_tensor("wo", [D, D], F32, kind="ExternalInput").ap()
    wg_d = nc.dram_tensor("wg3", [E * D, D], F32, kind="ExternalInput").ap()
    wu_d = nc.dram_tensor("wu3", [E * D, D], F32, kind="ExternalInput").ap()
    wd_d = nc.dram_tensor("wd3", [E * D, D], F32, kind="ExternalInput").ap()
    wr_d = nc.dram_tensor("wr", [D, E], F32, kind="ExternalInput").ap()
    temp_d = nc.dram_tensor("tempcol", [128, 1], F32, kind="ExternalInput").ap()
    cos_d = nc.dram_tensor("cos2", [128, T], BF16, kind="ExternalInput").ap()
    sin_d = nc.dram_tensor("sin2", [128, T], BF16, kind="ExternalInput").ap()
    sgn_d = nc.dram_tensor("sgncol", [128, 1], F32, kind="ExternalInput").ap()
    id_d = nc.dram_tensor("identb", [128, 128], BF16, kind="ExternalInput").ap()
    msk_d = nc.dram_tensor("maskt", [128, 1920], BF16, kind="ExternalInput").ap()
    kcol_d = nc.dram_tensor("kcol", [128, 4], F32, kind="ExternalInput").ap()
    out_d = nc.dram_tensor("out", [T, D], F32, kind="ExternalOutput").ap()
    ybuf_d = nc.dram_tensor("ybuf", [T, D], F32, kind="Internal").ap()
    x1f_d = nc.dram_tensor("x1f", [T, D], F32, kind="Internal").ap()
    tap_d = {}
    if "x1t" in taps:
        tap_d["x1t"] = nc.dram_tensor("tap_x1t", [128, KC * T], BF16, kind="ExternalOutput").ap()
    if "outh" in taps:
        tap_d["outh"] = nc.dram_tensor("tap_outh", [128, H * T], BF16, kind="ExternalOutput").ap()
    if "router" in taps:
        tap_d["router"] = nc.dram_tensor("tap_router", [128, 128], F32, kind="ExternalOutput").ap()
    if "x1" in taps:
        tap_d["x1"] = nc.dram_tensor("tap_x1", [T, D], F32, kind="ExternalOutput").ap()
    if "y" in taps:
        tap_d["y"] = nc.dram_tensor("tap_y", [T, D], F32, kind="ExternalOutput").ap()
    if "x0t" in taps:
        tap_d["x0t"] = nc.dram_tensor("tap_x0t", [128, KC * T], BF16, kind="ExternalOutput").ap()
    if "qkv" in taps:
        tap_d["qt"] = nc.dram_tensor("tap_qt", [128, T], BF16, kind="ExternalOutput").ap()
        tap_d["kt"] = nc.dram_tensor("tap_kt", [128, T], BF16, kind="ExternalOutput").ap()
        tap_d["vt"] = nc.dram_tensor("tap_vt", [128, T], BF16, kind="ExternalOutput").ap()
    if "probs" in taps:
        tap_d["probs"] = nc.dram_tensor("tap_probs", [128, 9216], BF16, kind="ExternalOutput").ap()
        tap_d["probst"] = nc.dram_tensor("tap_probst", [128, 6144], BF16, kind="ExternalOutput").ap()
        tap_d["stat"] = nc.dram_tensor("tap_stat", [128, 32], F32, kind="ExternalOutput").ap()

    s_pe = S.sem("pe"); s_act = S.sem("act"); s_dve = S.sem("dve")
    s_bulk = S.sem("bulk")
    s_gbulk = S.sem("gbulk")
    s_pool = S.sem("pool")
    s_wsl = [S.sem(f"wsl{i}") for i in range(4)]
    s_wgwu = [S.sem(f"wgwu{i}") for i in range(2)]
    s_wd = [S.sem(f"wd{i}") for i in range(3)]
    s_wo = S.sem("wo")
    s_x0s = [S.sem(f"x0s{i}") for i in range(4)]
    s_yw = S.sem("yw")
    s_yr = [S.sem(f"yr{i}") for i in range(2)]
    s_x1w = S.sem("x1w")
    s_x1r = S.sem("x1r")
    s_out = S.sem("sout")

    def pe(fn, waits=(), inc=False):
        return S.op("tensor", fn, waits, s_pe if inc else None)

    def dve(fn, waits=(), inc=False):
        w = list(waits) + [(s_dve, s_dve.n, True)]
        return S.op("vector", fn, w, s_dve)

    def act(fn, waits=(), inc=False):
        w = list(waits) + [(s_act, s_act.n, True)]
        return S.op("scalar", fn, w, s_act)

    def pool(fn, waits=()):
        w = list(waits) + [(s_pool, s_pool.n, True)]
        return S.op("gpsimd", fn, w, s_pool)

    last_dma = {}

    def _dma(eng, dst, src, sem, waits, tkey):
        w = list(waits)
        if tkey is None:
            tkey = dst.tensor.name if hasattr(dst, "tensor") else None
        if tkey is not None and tkey in last_dma:
            ps = last_dma[tkey]
            w.append((ps, ps.n, True))
        tok = S.op(eng, (lambda g, d=dst, s=src: g.dma_start(d, s)), w, sem, 16)
        if tkey is not None:
            last_dma[tkey] = sem
        return tok

    def gdma(dst, src, sem, waits=(), tkey=None):
        return _dma("gpsimd", dst, src, sem, waits, tkey)

    def sdma(dst, src, sem, waits=(), tkey=None):
        return _dma("sync", dst, src, sem, waits, tkey)

    with ExitStack() as ctx:
        for s in S.sems.values():
            s.h = ctx.enter_context(nc.semaphore(s.name))

        xT = ctx.enter_context(nc.sbuf_tensor("xT", [128, KC * T], BF16))
        identb = ctx.enter_context(nc.sbuf_tensor("identb_s", [128, 128], BF16))
        wrb = ctx.enter_context(nc.sbuf_tensor("wrb", [128, KC * E], BF16))
        rtr = ctx.enter_context(nc.sbuf_tensor("rtr", [128, 288], F32))
        stat = ctx.enter_context(nc.sbuf_tensor("stat", [128, 64], F32))
        sgncol = ctx.enter_context(nc.sbuf_tensor("sgncol_s", [128, 1], F32))
        tempcol = ctx.enter_context(nc.sbuf_tensor("tempcol_s", [128, 1], F32))
        kcol = ctx.enter_context(nc.sbuf_tensor("kcol_s", [128, 4], F32))
        ctxA = ExitStack()
        ctx.callback(ctxA.close)
        big = ctxA.enter_context(nc.sbuf_tensor("big", [128, H * T], BF16))

        psAB = [ctx.enter_context(nc.psum_tensor(n, [128, 1536], F32)) for n in ("psA", "psB")]

        def psx(par, reg):
            return psAB[par][:, reg * 512:(reg + 1) * 512]

        ps = [psx(0, 0), psx(1, 0), psx(0, 1), psx(1, 1), psx(0, 2), psx(1, 2)]
        ptr = ctx.enter_context(nc.psum_tensor("ptr", [128, 1024], BF16))
        psav = ctx.enter_context(nc.psum_tensor("psav", [128, 512], F32))

        bank_free = {}

        def mm_group(psum_ap, pairs, extra_waits=(), bank=None):
            n = len(pairs)
            tok = None
            for i, (l, r) in enumerate(pairs):
                w = list(extra_waits) if i == 0 else []
                if i == 0 and bank is not None and bank in bank_free:
                    bf = bank_free[bank]
                    w.extend(bf if isinstance(bf, list) else [bf])
                tok = S.op("tensor",
                           (lambda t, o=psum_ap, a=l, b=r, st=(i == 0), sp=(i == n - 1):
                            t.matmul(o, a, b, start=st, stop=sp)),
                           w, s_pe if i == n - 1 else None)
            return tok

        sdma(identb[:, :], id_d[:, :], s_bulk)
        sdma(sgncol[:, :], sgn_d[:, :], s_bulk)
        sdma(tempcol[:, :], temp_d[:, :], s_bulk)
        bigv = big.ap().rearrange("p (tg d) -> p tg d", d=D)
        x0v = x0_d.rearrange("(tg p) d -> p tg d", p=128)
        gdma(bigv[:, 0:8], x0v[:, 0:8], s_gbulk)
        gdma(bigv[:, 8:16], x0v[:, 8:16], s_gbulk)
        gdma(wrb.ap().rearrange("p (kc e) -> p kc e", e=E),
             wr_d.rearrange("(kc p) e -> p kc e", p=128), s_gbulk)
        sdma(kcol[:, :], kcol_d[:, :], s_bulk)

        ctxH = ExitStack()
        ctx.callback(ctxH.close)
        cos2 = ctxH.enter_context(nc.sbuf_tensor("cos2_s", [128, T], BF16))
        sin2 = ctxH.enter_context(nc.sbuf_tensor("sin2_s", [128, T], BF16))
        maskt = ctxH.enter_context(nc.sbuf_tensor("maskt_s", [128, 1920], BF16))
        qT = ctxH.enter_context(nc.sbuf_tensor("qT", [128, T], BF16))
        kT = ctxH.enter_context(nc.sbuf_tensor("kT", [128, T], BF16))
        vT = ctxH.enter_context(nc.sbuf_tensor("vT", [128, T], BF16))
        probs = ctxH.enter_context(nc.sbuf_tensor("probs", [128, 9216], BF16))
        probsT = ctxH.enter_context(nc.sbuf_tensor("probsT", [128, 6144], BF16))
        ropet = ctxH.enter_context(nc.sbuf_tensor("ropet", [128, 1024], BF16))
        zerob = ctxH.enter_context(nc.sbuf_tensor("zerob", [128, 128], BF16))
        wsl = [ctxH.enter_context(nc.sbuf_tensor(f"wsl{i}", [128, KC * 128], BF16))
               for i in range(4)]

        sdma(cos2[:, :], cos_d[:, :], s_bulk)
        sdma(sin2[:, :], sin_d[:, :], s_bulk)
        sdma(maskt[:, :], msk_d[:, :], s_bulk)
        tok_z = dve(lambda v, o=zerob[:, :]: v.memset(o, 0.0), [], inc=True)
        bulk_w = [(s_bulk, s_bulk.n), (s_gbulk, s_gbulk.n), (s_dve, tok_z)]

        xTv = xT.ap().rearrange("p (kc t) -> p kc t", t=T)
        ptrv = ptr.ap().rearrange("p (b t) -> p b t", t=128)
        ptr_free = None
        x0T_done = None
        for tg in range(TG):
            for g8 in range(2):
                tok_tr = None
                for j in range(8):
                    dc = g8 * 8 + j
                    w = list(bulk_w)
                    if j == 0 and ptr_free is not None:
                        w.append(ptr_free)
                    tok_tr = pe(lambda t, o=ptr[:, j * 128:(j + 1) * 128],
                                i=bigv[:, tg, dc * 128:(dc + 1) * 128], idn=identb[:, :]:
                                t.transpose(o, i, idn), w, inc=(j == 7))
                dst = xTv[:, g8 * 8:(g8 + 1) * 8, tg * 128:(tg + 1) * 128]
                x0T_done = dve(lambda v, d=dst, s=ptrv: v.tensor_copy(d, s),
                               [(s_pe, tok_tr)], inc=True)
                ptr_free = (s_dve, x0T_done)
        x0T_w = [(s_dve, x0T_done)]
        tok_last_x0big_pe = s_pe.n

        if "x0t" in tap_d:
            sdma(tap_d["x0t"][:, :], xT[:, :], s_out, x0T_w)

        wslv = [w.ap().rearrange("p (kc m) -> p kc m", m=128) for w in wsl]
        slot_ready = [None] * 4
        slot_lastuse = [None] * 4
        nslice = H * 3

        def issue_slice_load(j):
            hh, m = divmod(j, 3)
            mat = (wq_d, wk_d, wv_d)[m]
            slot = j % 4
            free_w = [slot_lastuse[slot]] if slot_lastuse[slot] else []
            src = mat[:, hh * 128:(hh + 1) * 128]
            free_f = [(w[0], w[1], True) for w in free_w]
            gdma(wslv[slot][:, 0:8], src[0:1024, :].rearrange("(kc p) m -> p kc m", p=128),
                 s_wsl[slot], free_f)
            tok = gdma(wslv[slot][:, 8:16], src[1024:2048, :].rearrange("(kc p) m -> p kc m", p=128),
                       s_wsl[slot], free_f)
            slot_ready[slot] = (s_wsl[slot], tok)

        for j in range(min(4, nslice)):
            issue_slice_load(j)

        qkT_free = None
        vT_free = None
        probs_free = None
        probsT_free = None
        psav_free = None
        rope_prev = None
        outh_tok = None
        qT_ready = kT_ready = None

        maskF = maskt[:, 3 * 128:4 * 128]
        maskD = maskt[:, 11 * 128:12 * 128]
        probs_free = [None] * 8
        stat_free = [None] * 8
        pt_last_act = None
        pt_memset_tok = None

        for h in range(H):
            for m, (arena, scale) in enumerate(((qT, QSCALE), (kT, 1.0))):
                slot = (h * 3 + m) % 4
                rw = [slot_ready[slot]]
                tok_rope = None
                for tb in range(4):
                    bank = f"ps{tb % 2}"
                    pairs = [(wslv[slot][:, kc, :], xTv[:, kc, tb * 512:(tb + 1) * 512])
                             for kc in range(KC)]
                    tok_mm = mm_group(ps[tb % 2][:, :], pairs, rw + x0T_w, bank=bank)
                    slot_lastuse[slot] = (s_pe, tok_mm)
                    tcols = slice(tb * 512, (tb + 1) * 512)
                    q_raw = ropet[:, 0:512]
                    t_sw = ropet[:, 512:1024]
                    wv_ev = [(s_pe, tok_mm)]
                    if rope_prev is not None:
                        wv_ev.append(rope_prev)
                    tok_ev = act(lambda a, o=q_raw, i=ps[tb % 2][:, :], sc=scale:
                                 a.activation(o, i, AF.Copy, scale=sc), wv_ev, inc=True)
                    bank_free[bank] = [(s_act, tok_ev)]
                    dve(lambda v, o=t_sw[0:64, :], i=q_raw[64:128, :]: v.tensor_copy(o, i),
                        [(s_act, tok_ev)] + ([rope_prev] if rope_prev else []))
                    tok_sw = dve(lambda v, o=t_sw[64:128, :], i=q_raw[0:64, :]:
                                 v.tensor_copy(o, i), [], inc=True)
                    dve(lambda v, o=t_sw, i=sin2[:, tcols]:
                        v.tensor_tensor(o, o, i, ALU.mult), bulk_w)
                    dve(lambda v, o=q_raw, i=cos2[:, tcols]: v.tensor_tensor(o, o, i, ALU.mult),
                        [])
                    ws = [qkT_free] if qkT_free else []
                    tok_rope = dve(lambda v, o=arena[:, tcols], i0=t_sw, s=sgncol[:, :], i1=q_raw:
                                   v.scalar_tensor_tensor(o, i0, s, i1, ALU.mult, ALU.add),
                                   ws + bulk_w, inc=True)
                    rope_prev = (s_dve, tok_rope)
                if m == 0:
                    qT_ready = (s_dve, tok_rope)
                else:
                    kT_ready = (s_dve, tok_rope)
            slot = (h * 3 + 2) % 4
            rw = [slot_ready[slot]]
            vtok_ev = None
            for tb in range(4):
                bank = f"ps{tb % 2}"
                pairs = [(wslv[slot][:, kc, :], xTv[:, kc, tb * 512:(tb + 1) * 512])
                         for kc in range(KC)]
                tok_mm = mm_group(ps[tb % 2][:, :], pairs, rw + x0T_w, bank=bank)
                slot_lastuse[slot] = (s_pe, tok_mm)
                wv_ev = [(s_pe, tok_mm)]
                if vT_free is not None:
                    wv_ev.append(vT_free)
                vtok_ev = act(lambda a, o=vT[:, tb * 512:(tb + 1) * 512], i=ps[tb % 2][:, :]:
                              a.activation(o, i, AF.Copy), wv_ev, inc=True)
                bank_free[bank] = [(s_act, vtok_ev)]
            vT_feat = (s_act, vtok_ev)
            for jn in range(h * 3 + 3, min(h * 3 + 6, nslice)):
                issue_slice_load(jn)
            tok_vtok = None
            for g8 in range(2):
                tok_tr = None
                for jj in range(8):
                    i = g8 * 8 + jj
                    w = [vT_feat]
                    if jj == 0 and ptr_free is not None:
                        w.append(ptr_free)
                    tok_tr = pe(lambda t, o=ptr[:, jj * 128:(jj + 1) * 128],
                                i_=vT[:, i * 128:(i + 1) * 128], idn=identb[:, :]:
                                t.transpose(o, i_, idn), w, inc=(jj == 7))
                dst = vT.ap().rearrange("p (b t) -> p b t", t=128)[:, g8 * 8:(g8 + 1) * 8, :]
                tok_vtok = act(lambda a, d=dst, s=ptrv: a.activation(d, s, AF.Copy),
                               [(s_pe, tok_tr)], inc=True)
                ptr_free = (s_act, tok_vtok)
            v_tok_ready = (s_act, tok_vtok)

            SKEW = 6
            units = [(qt, qi) for qt in range(4) for qi in range(4)]
            uinfo = [None] * 16

            def p1(u):
                nonlocal qkT_free
                qt, qi = units[u]
                qg = qt * 4 + qi
                w0b = max(0, qg - 8)
                nwb = qg + 1 - w0b
                wcols = nwb * 128
                par = u % 2
                pspar = psAB[par]
                pb = u % 8
                prb = probs[:, pb * 1152: pb * 1152 + wcols]
                st = pb * 4
                bw = list(stat_free[pb] or [])
                for reg in range(3):
                    k = f"ps{reg * 2 + par}"
                    if k in bank_free:
                        bf = bank_free[k]
                        bw += bf if isinstance(bf, list) else [bf]
                off = 0
                first = True
                tok_s = None
                while off < wcols:
                    cw = min(512, wcols - off)
                    tok_s = S.op("tensor",
                                 (lambda t, o=pspar[:, off:off + cw],
                                  a=qT[:, qg * 128:(qg + 1) * 128],
                                  b=kT[:, w0b * 128 + off: w0b * 128 + off + cw]:
                                  t.matmul(o, a, b, start=True, stop=True)),
                                 ([qT_ready, kT_ready] + bw) if first else [], s_pe)
                    first = False
                    off += cw
                qkT_free = (s_pe, tok_s)
                if qg >= 8:
                    S.op("tensor", (lambda t, o=pspar[:, 0:128], a=identb[:, :], b=maskF:
                                    t.matmul(o, a, b, start=False, stop=True)), bulk_w, None)
                tok_s = S.op("tensor",
                             (lambda t, o=pspar[:, wcols - 128:wcols], a=identb[:, :], b=maskD:
                              t.matmul(o, a, b, start=False, stop=True)), bulk_w, s_pe)
                tok_max = dve(lambda v, o=stat[:, st:st + 1], i=pspar[:, 0:wcols]:
                              v.reduce_max(o, i, AX.X, negate=True), [(s_pe, tok_s)], inc=True)
                pw = [(s_dve, tok_max)]
                if probs_free[pb]:
                    pw.append(probs_free[pb])
                tok_exp = act(lambda a, o=prb, i=pspar[:, 0:wcols], b=stat[:, st:st + 1]:
                              a.activation(o, i, AF.Exp, bias=b), pw, inc=True)
                stat_free[pb] = [(s_act, tok_exp)]
                for reg in range(3):
                    bank_free[f"ps{reg * 2 + par}"] = [(s_act, tok_exp)]
                tok_rs = dve(lambda v, o=stat[:, st + 1:st + 2], i=prb:
                             v.reduce_sum(o, i, AX.X), [(s_act, tok_exp)], inc=True)
                tok_rc = dve(lambda v, o=stat[:, st + 2:st + 3], i=stat[:, st + 1:st + 2]:
                             v.reciprocal(o, i), [(s_dve, tok_rs)], inc=True)
                tok_nrm = dve(lambda v, o=prb, s=stat[:, st + 2:st + 3]:
                              v.tensor_scalar(o, o, s, None, ALU.mult),
                              [(s_dve, tok_rc)], inc=True)
                uinfo[u] = (qt, qi, qg, w0b, nwb, pb, tok_nrm)

            def p2(u):
                nonlocal ptr_free, pt_last_act, pt_memset_tok
                qt, qi, qg, w0b, nwb, pb, tok_nrm = uinfo[u]
                s0b = max(0, qt * 4 - 8)
                wb = qt * 4 + 4 - s0b
                prbase = pb * 1152
                pfw = [probsT_free] if probsT_free else []
                for p in range(wb):
                    blk = s0b + p
                    if blk < w0b or blk > qg:
                        pt_memset_tok = dve(
                            lambda v, o=probsT[:, p * 512 + qi * 128: p * 512 + (qi + 1) * 128]:
                            v.memset(o, 0.0), pfw, inc=True)
                for b0 in range(0, nwb, 8):
                    bn = min(8, nwb - b0)
                    tok_tr = None
                    for jj in range(bn):
                        w = [(s_dve, tok_nrm)]
                        if jj == 0 and ptr_free is not None:
                            w.append(ptr_free)
                        tok_tr = pe(lambda t, o=ptr[:, jj * 128:(jj + 1) * 128],
                                    i_=probs[:, prbase + (b0 + jj) * 128:
                                             prbase + (b0 + jj + 1) * 128],
                                    idn=identb[:, :]:
                                    t.transpose(o, i_, idn), w, inc=(jj == bn - 1))
                    p0 = w0b - s0b + b0
                    dst = probsT.ap().rearrange("p (b q) -> p b q", q=512)[
                        :, p0:p0 + bn, qi * 128:(qi + 1) * 128]
                    w2 = [(s_pe, tok_tr)] + pfw
                    tok_pt = act(lambda a, d=dst, s=ptrv[:, 0:bn, :]:
                                 a.activation(d, s, AF.Copy), w2, inc=True)
                    ptr_free = (s_act, tok_pt)
                    pt_last_act = tok_pt
                probs_free[pb] = (s_pe, tok_tr)

            def do_av(qt):
                nonlocal vT_free, probsT_free, psav_free, outh_tok
                s0b = max(0, qt * 4 - 8)
                wb = qt * 4 + 4 - s0b
                pairs = [(vT[:, (s0b + p) * 128:(s0b + p + 1) * 128],
                          probsT[:, p * 512:(p + 1) * 512]) for p in range(wb)]
                ew = [(s_act, pt_last_act), v_tok_ready]
                if pt_memset_tok is not None:
                    ew.append((s_dve, pt_memset_tok))
                if psav_free is not None:
                    ew.append(psav_free)
                tok_av = mm_group(psav[:, :], pairs, ew, bank=None)
                vT_free = (s_pe, tok_av)
                probsT_free = (s_pe, tok_av)
                w3 = [(s_pe, tok_av)]
                if h == 0 and qt == 0:
                    w3.append((s_pe, tok_last_x0big_pe))
                outh_tok = act(lambda a, o=big[:, h * T + qt * 512: h * T + (qt + 1) * 512],
                               i=psav[:, :]: a.activation(o, i, AF.Copy), w3, inc=True)
                psav_free = (s_act, outh_tok)

            for step in range(16 + SKEW):
                if step < 16:
                    p1(step)
                if step >= SKEW:
                    u = step - SKEW
                    p2(u)
                    if units[u][1] == 3:
                        do_av(units[u][0])
        outh_w = [(s_act, outh_tok)]

        if "outh" in tap_d:
            sdma(tap_d["outh"][:, :], big[:, :], s_out, outh_w)
        if "qt" in tap_d:
            sdma(tap_d["qt"][:, :], qT[:, :], s_out, outh_w)
            sdma(tap_d["kt"][:, :], kT[:, :], s_out, outh_w)
            sdma(tap_d["vt"][:, :], vT[:, :], s_out, outh_w)
        if "probs" in tap_d:
            sdma(tap_d["probs"][:, :], probs[:, :], s_out, outh_w)
            sdma(tap_d["probst"][:, :], probsT[:, :], s_out, outh_w)
            sdma(tap_d["stat"][:, :], stat[:, :], s_out, outh_w)

        gateA = [(s_pe, s_pe.n), (s_act, s_act.n), (s_dve, s_dve.n)]
        ctxH.close()

        ctxW = ExitStack()
        ctx.callback(ctxW.close)
        woslab = [ctxW.enter_context(nc.sbuf_tensor(f"woslab{i}", [128, KC * 256], BF16))
                  for i in range(2)]
        x0sl = [ctxW.enter_context(nc.sbuf_tensor(f"x0sl{i}", [128, 512], F32))
                for i in range(4)]
        ysl = ctxW.enter_context(nc.sbuf_tensor("ysl", [128, 4 * 512], F32))
        yrow = [ctxW.enter_context(nc.sbuf_tensor(f"yrow{i}", [128, D], F32))
                for i in range(2)]
        x1fr = ctxW.enter_context(nc.sbuf_tensor("x1fr", [128, 2 * D], F32))
        x1bf = ctxW.enter_context(nc.sbuf_tensor("x1bf", [128, 2 * D], BF16))

        woslabv = [w_.ap().rearrange("p (hc d) -> p hc d", d=256) for w_ in woslab]
        wo_lastuse = [None, None]
        x0_lastuse = [None] * 4
        ysl_lastuse = [None] * 4
        wo_tok = [None, None]

        def load_woslab(dt):
            slot = dt % 2
            w = list(gateA)
            if wo_lastuse[slot]:
                w.append((wo_lastuse[slot][0], wo_lastuse[slot][1], True))
            src = wo_d[:, dt * 256:(dt + 1) * 256]
            gdma(woslabv[slot][:, 0:8], src[0:1024, :].rearrange("(hc p) d -> p hc d", p=128),
                 s_wo, w)
            wo_tok[slot] = gdma(woslabv[slot][:, 8:16],
                                src[1024:2048, :].rearrange("(hc p) d -> p hc d", p=128), s_wo, w)

        load_woslab(0)
        load_woslab(1)
        for dt in range(8):
            slot = dt % 2
            wo_w = [(s_wo, wo_tok[slot])]
            for tg in range(TG):
                sl = tg % 4
                w = list(gateA) + ([x0_lastuse[sl]] if x0_lastuse[sl] else [])
                tok_x0 = sdma(x0sl[sl][:, 0:256],
                              x0_d[tg * 128:(tg + 1) * 128, dt * 256:(dt + 1) * 256],
                              s_x0s[sl], w)
                bank = f"ps{tg % 2}"
                pairs = [(big[:, hc * T + tg * 128: hc * T + tg * 128 + 128],
                          woslabv[slot][:, hc, :]) for hc in range(H)]
                tok_mm = mm_group(ps[tg % 2][:, 0:256], pairs, outh_w + wo_w, bank=bank)
                wo_lastuse[slot] = (s_pe, tok_mm)
                w2 = [(s_pe, tok_mm), (s_x0s[sl], tok_x0)]
                if ysl_lastuse[sl]:
                    w2.append(ysl_lastuse[sl])
                tok_y = dve(lambda v, o=ysl[:, sl * 512:sl * 512 + 256], a=ps[tg % 2][:, 0:256],
                            b=x0sl[sl][:, 0:256]:
                            v.tensor_tensor(o, a, b, ALU.add), w2, inc=True)
                bank_free[bank] = (s_dve, tok_y)
                x0_lastuse[sl] = (s_dve, tok_y)
                tok_st = sdma(ybuf_d[tg * 128:(tg + 1) * 128, dt * 256:(dt + 1) * 256],
                              ysl[:, sl * 512:sl * 512 + 256], s_yw, [(s_dve, tok_y)])
                ysl_lastuse[sl] = (s_yw, tok_st)
            if dt + 2 < 8:
                load_woslab(dt + 2)
        yw_all = [(s_yw, s_yw.n)]

        if "y" in tap_d:
            for tg in range(TG):
                sdma(tap_d["y"][tg * 128:(tg + 1) * 128, :], ybuf_d[tg * 128:(tg + 1) * 128, :],
                     s_out, yw_all)

        yr_lastuse = [None, None]
        x1fr_lastuse = [None, None]
        x1bf_lastuse = [None, None]
        x1T_evac = []
        c0 = 40
        for tg in range(TG):
            sl = tg % 2
            yr = yrow[sl][:, :]
            jnk = x1fr[:, sl * D:(sl + 1) * D]
            w = list(yw_all)
            if yr_lastuse[sl]:
                w += yr_lastuse[sl]
            tok_ld = sdma(yr, ybuf_d[tg * 128:(tg + 1) * 128, :], s_yr[sl], w)
            ld_w = [(s_yr[sl], tok_ld)]
            tok_s1 = dve(lambda v, o=stat[:, c0:c0 + 1], i=yr: v.reduce_sum(o, i, AX.X),
                         ld_w, inc=True)
            tok_ssq = dve(lambda v, o=jnk, a=yr: v.tensor_tensor(o, a, a, ALU.mult),
                          ld_w + ([x1fr_lastuse[sl]] if x1fr_lastuse[sl] else []), inc=True)
            tok_s2 = dve(lambda v, o=stat[:, c0 + 1:c0 + 2], i=jnk: v.reduce_sum(o, i, AX.X),
                         [(s_dve, tok_ssq)], inc=True)
            tok_m = dve(lambda v, o=stat[:, c0 + 2:c0 + 3], i=stat[:, c0:c0 + 1]:
                        v.tensor_scalar(o, i, 1.0 / D, None, ALU.mult),
                        [(s_dve, tok_s1)], inc=True)
            tok_e = dve(lambda v, o=stat[:, c0 + 3:c0 + 4], i=stat[:, c0 + 1:c0 + 2]:
                        v.tensor_scalar(o, i, 1.0 / D, None, ALU.mult),
                        [(s_dve, tok_s2)], inc=True)
            tok_mq = dve(lambda v, o=stat[:, c0 + 4:c0 + 5], a=stat[:, c0 + 2:c0 + 3]:
                         v.tensor_tensor(o, a, a, ALU.mult), [(s_dve, tok_m)], inc=True)
            tok_var = dve(lambda v, o=stat[:, c0 + 3:c0 + 4], a=stat[:, c0 + 3:c0 + 4],
                          b=stat[:, c0 + 4:c0 + 5]: v.tensor_tensor(o, a, b, ALU.subtract),
                          [(s_dve, tok_e), (s_dve, tok_mq)], inc=True)
            act(lambda a, o=stat[:, c0 + 4:c0 + 5], i=stat[:, c0 + 3:c0 + 4],
                b=kcol[:, 1:2]: a.activation(o, i, AF.Ln, bias=b),
                [(s_dve, tok_var)] + bulk_w)
            tok_rstd = act(lambda a, o=stat[:, c0 + 5:c0 + 6], i=stat[:, c0 + 4:c0 + 5]:
                           a.activation(o, i, AF.Exp, scale=-0.5), [], inc=True)
            tok_nmr = dve(lambda v, o=stat[:, c0 + 6:c0 + 7], i=stat[:, c0 + 2:c0 + 3],
                          r=stat[:, c0 + 5:c0 + 6]:
                          v.scalar_tensor_tensor(o, i, -1.0, r, ALU.mult, ALU.mult),
                          [(s_act, tok_rstd), (s_dve, tok_m)], inc=True)
            tok_x1f = dve(lambda v, o=jnk, i=yr, r=stat[:, c0 + 5:c0 + 6], n=stat[:, c0 + 6:c0 + 7]:
                          v.tensor_scalar(o, i, r, n, ALU.mult, ALU.add),
                          [(s_dve, tok_ssq), (s_dve, tok_nmr)], inc=True)
            wbf = [(s_dve, tok_nmr)] + ([x1bf_lastuse[sl]] if x1bf_lastuse[sl] else [])
            tok_x1b = act(lambda a, o=x1bf[:, sl * D:(sl + 1) * D], i=yr,
                          s=stat[:, c0 + 5:c0 + 6], b=stat[:, c0 + 6:c0 + 7]:
                          a.activation(o, i, AF.Identity, scale=s, bias=b), wbf, inc=True)
            yr_lastuse[sl] = [(s_act, tok_x1b), (s_dve, tok_x1f)]
            tok_w = sdma(x1f_d[tg * 128:(tg + 1) * 128, :], jnk, s_x1w, [(s_dve, tok_x1f)])
            x1fr_lastuse[sl] = (s_x1w, tok_w)
            tok_tr = None
            for g8 in range(2):
                for jj in range(8):
                    dc = g8 * 8 + jj
                    w = [(s_act, tok_x1b)]
                    if jj == 0 and ptr_free is not None:
                        w.append(ptr_free)
                    tok_tr = pe(lambda t, o=ptr[:, jj * 128:(jj + 1) * 128],
                                i_=x1bf[:, sl * D + dc * 128: sl * D + (dc + 1) * 128],
                                idn=identb[:, :]: t.transpose(o, i_, idn), w, inc=(jj == 7))
                dst = xTv[:, g8 * 8:(g8 + 1) * 8, tg * 128:(tg + 1) * 128]
                tok_ev = dve(lambda v, d=dst, s=ptrv: v.tensor_copy(d, s),
                             [(s_pe, tok_tr)], inc=True)
                ptr_free = (s_dve, tok_ev)
                x1T_evac.append(tok_ev)
            x1bf_lastuse[sl] = (s_pe, tok_tr)
        x1T_w = [(s_dve, max(x1T_evac))]
        x1w_all = [(s_x1w, s_x1w.n)]

        if "x1t" in tap_d:
            sdma(tap_d["x1t"][:, :], xT[:, :], s_out, x1T_w)
        if "x1" in tap_d:
            for tg in range(TG):
                sdma(tap_d["x1"][tg * 128:(tg + 1) * 128, :], x1f_d[tg * 128:(tg + 1) * 128, :],
                     s_out, x1w_all)

        if upto == "ln1":
            S.op("sync", lambda s, n=s_out.n: s.wait_ge(s_out.h, n), x1w_all + yw_all, None)
            S.check()
            _emit_block(nc, S)
            return nc

        gateL = [(s_pe, s_pe.n), (s_act, s_act.n), (s_dve, s_dve.n)] + list(x1w_all)
        gateB = [(s_pe, s_pe.n), (s_act, s_act.n), (s_dve, s_dve.n)] + list(yw_all) + list(x1w_all)
        wrbv = wrb.ap().rearrange("p (kc e) -> p kc e", e=E)
        tok_rmm = None
        for tg in range(TG):
            pairs = [(xTv[:, kc, tg * 128:(tg + 1) * 128], wrbv[:, kc, :]) for kc in range(KC)]
            ew = x1T_w + bulk_w
            if tg == 0:
                ew = ew + gateB
                if "ps0" in bank_free:
                    ew.append(bank_free["ps0"])
            tok_rmm = mm_group(ps[0][:, tg * 8:(tg + 1) * 8], pairs, ew, bank=None)
        tok_rt = dve(lambda v, o=stat[:, 32:33], i=tempcol[:, :]: v.reciprocal(o, i),
                     bulk_w, inc=True)
        tok_inh = act(lambda a, o=rtr[:, 0:128], i=ps[0][:, 0:128], s=stat[:, 32:33]:
                      a.activation(o, i, AF.Sigmoid, scale=s),
                      [(s_pe, tok_rmm), (s_dve, tok_rt)] + gateB, inc=True)
        bank_free["ps0"] = (s_act, tok_inh)
        dist = ysl[:, 0:128]
        wz = ysl[:, 128:256]
        fbv = ysl[:, 256:384]
        wv_ = ysl[:, 384:512]
        fb2 = ysl[:, 512:640]
        fbw = ysl[:, 640:768]
        t1v = ysl[:, 768:896]
        wsc = gateB + list(yw_all)
        tok_dist = act(lambda a, o=dist, i=rtr[:, 0:128], b=kcol[:, 0:1]:
                       a.activation(o, i, AF.Abs, bias=b),
                       [(s_act, tok_inh)] + wsc + bulk_w, inc=True)
        act(lambda a, o=wz, i=dist: a.activation(o, i, AF.Exp, scale=-10.0), [])
        tok_fb = act(lambda a, o=fbv, i=dist: a.activation(o, i, AF.Exp, scale=float(-1.0 / 0.3)),
                     [], inc=True)
        dve(lambda v, o=t1v, i=rtr[:, 0:128]: v.tensor_scalar(o, i, GOLDEN_LOWER, None, ALU.is_ge),
            [(s_act, tok_fb)] + wsc)
        dve(lambda v, o=wv_, a=wz, b=t1v: v.tensor_tensor(o, a, b, ALU.mult), [])
        dve(lambda v, o=t1v, i=rtr[:, 0:128]: v.tensor_scalar(o, i, GOLDEN_UPPER, None, ALU.is_le), [])
        tok_w = dve(lambda v, o=wv_, a=wv_, b=t1v: v.tensor_tensor(o, a, b, ALU.mult), [], inc=True)
        wtok_toks = []
        r0 = 48
        for tg in range(TG):
            s8 = slice(tg * 8, (tg + 1) * 8)
            t = dve(lambda v, o=stat[:, r0:r0 + 1], i=wv_[:, s8]: v.reduce_sum(o, i, AX.X),
                    [(s_dve, tok_w)], inc=True)
            t = dve(lambda v, o=stat[:, r0:r0 + 1], i=stat[:, r0:r0 + 1]:
                    v.tensor_scalar(o, i, 1e-8, None, ALU.is_lt), [(s_dve, t)], inc=True)
            tok_ne = t
            t = dve(lambda v, o=stat[:, r0 + 1:r0 + 2], i=fbv[:, s8]: v.reduce_max(o, i, AX.X),
                    [], inc=True)
            t = dve(lambda v, o=fb2[:, s8], i=fbv[:, s8], m=stat[:, r0 + 1:r0 + 2]:
                    v.tensor_scalar(o, i, m, None, ALU.is_ge), [(s_dve, t)], inc=True)
            t = dve(lambda v, o=fb2[:, s8], i0=fb2[:, s8], i1=fbv[:, s8]:
                    v.scalar_tensor_tensor(o, i0, -2.0, i1, ALU.mult, ALU.add),
                    [(s_dve, t)], inc=True)
            t = dve(lambda v, o=stat[:, r0 + 1:r0 + 2], i=fb2[:, s8]: v.reduce_max(o, i, AX.X),
                    [(s_dve, t)], inc=True)
            t = dve(lambda v, o=fb2[:, s8], i=fbv[:, s8], m=stat[:, r0 + 1:r0 + 2]:
                    v.tensor_scalar(o, i, m, None, ALU.is_ge), [(s_dve, t)], inc=True)
            t = dve(lambda v, o=fbw[:, s8], a=fbv[:, s8], b=fb2[:, s8]:
                    v.tensor_tensor(o, a, b, ALU.mult), [(s_dve, t)], inc=True)
            t = dve(lambda v, o=stat[:, r0 + 1:r0 + 2], i=fbw[:, s8]: v.reduce_sum(o, i, AX.X),
                    [(s_dve, t)], inc=True)
            t = dve(lambda v, o=stat[:, r0 + 1:r0 + 2], i=stat[:, r0 + 1:r0 + 2]:
                    v.tensor_scalar(o, i, 1e-8, None, ALU.max), [(s_dve, t)], inc=True)
            t = dve(lambda v, o=stat[:, r0 + 2:r0 + 3], i=stat[:, r0 + 1:r0 + 2]:
                    v.reciprocal(o, i), [(s_dve, t)], inc=True)
            t = dve(lambda v, o=fbw[:, s8], i=fbw[:, s8], m=stat[:, r0 + 2:r0 + 3]:
                    v.tensor_scalar(o, i, m, None, ALU.mult), [(s_dve, t)], inc=True)
            t = dve(lambda v, o=t1v[:, s8], a=fbw[:, s8], b=wv_[:, s8]:
                    v.tensor_tensor(o, a, b, ALU.subtract), [(s_dve, t)], inc=True)
            t = dve(lambda v, o=t1v[:, s8], i0=t1v[:, s8], m=stat[:, r0:r0 + 1], i1=wv_[:, s8]:
                    v.scalar_tensor_tensor(o, i0, m, i1, ALU.mult, ALU.add),
                    [(s_dve, t), (s_dve, tok_ne)], inc=True)
            t = dve(lambda v, o=stat[:, r0 + 1:r0 + 2], i=t1v[:, s8]: v.reduce_sum(o, i, AX.X),
                    [(s_dve, t)], inc=True)
            t = dve(lambda v, o=stat[:, r0 + 1:r0 + 2], i=stat[:, r0 + 1:r0 + 2]:
                    v.tensor_scalar(o, i, 1e-8, None, ALU.max), [(s_dve, t)], inc=True)
            t = dve(lambda v, o=stat[:, r0 + 2:r0 + 3], i=stat[:, r0 + 1:r0 + 2]:
                    v.reciprocal(o, i), [(s_dve, t)], inc=True)
            tok_wt = dve(lambda v, o=rtr[:, 160 + tg * 8:160 + (tg + 1) * 8], i=t1v[:, s8],
                         m=stat[:, r0 + 2:r0 + 3]: v.tensor_scalar(o, i, m, None, ALU.mult),
                         [(s_dve, t)], inc=True)
            wtok_toks.append(tok_wt)
        wtok_w = [(s_dve, max(wtok_toks))]

        if "router" in tap_d:
            sdma(tap_d["router"][:, :], rtr[:, 160:288], s_out, wtok_w)

        if upto == "router":
            S.op("sync", lambda s, n=s_out.n: s.wait_ge(s_out.h, n),
                 x1w_all + yw_all + wtok_w, None)
            S.check()
            _emit_block(nc, S)
            return nc

        ctxW.close()
        ctxA.close()
        ctxM = ExitStack()
        ctx.callback(ctxM.close)
        wgu_t = [ctxM.enter_context(nc.sbuf_tensor(f"wgu{i}", [128, 2 * 16 * FT], BF16))
                 for i in range(2)]
        wd_t = [ctxM.enter_context(nc.sbuf_tensor(f"wdt{i}", [128, 4 * DT], BF16))
                for i in range(3)]
        h_t = ctxM.enter_context(nc.sbuf_tensor("h_t", [128, 8 * TT], BF16))
        x1row = ctxM.enter_context(nc.sbuf_tensor("x1row", [128, D], F32))
        y2x = ctxM.enter_context(nc.sbuf_tensor("y2x", [128, D], F32))
        out_acc = ctxM.enter_context(nc.sbuf_tensor("out_acc", [128, 8 * D], F32))

        gateC = list(gateL)
        accgate = list(gateL) + list(yw_all) + wtok_w

        def h_cols(fc_local, t0, tn):
            return h_t[:, fc_local * TT + t0: fc_local * TT + t0 + tn]

        def wgu_tile(slot, mat, kc, fb):
            b = mat * 4096 + kc * FT + fb * 128
            return wgu_t[slot][:, b:b + 128]

        def wgu_dst(slot, mat, kc0):
            b = mat * 4096 + kc0 * FT
            return wgu_t[slot].ap()[:, b:b + 8 * FT].rearrange("p (kc f) -> p kc f", f=FT)

        def wd_tile(slot, fc_local):
            return wd_t[slot][:, fc_local * DT: (fc_local + 1) * DT]

        def wd_dst(slot):
            return wd_t[slot].ap()[:, :].rearrange("p (fc d) -> p fc d", d=DT)

        wd_slot_ctr = [0]

        gu_group = 0
        d_group = 0
        wgwu_lastuse = [None, None]
        wd_lastuse = [None, None, None]
        h_lastuse = None
        x1r_lastuse = None
        acc_started = False

        def load_wgwu(e, half, sidx, slot):
            f0 = half * 1024 + sidx * FT
            w = list(gateC)
            if wgwu_lastuse[slot]:
                w.append((wgwu_lastuse[slot][0], wgwu_lastuse[slot][1], True))
            tok = None
            for mat, md in ((0, wg_d), (1, wu_d)):
                src = md[e * D:(e + 1) * D, f0:f0 + FT]
                gdma(wgu_dst(slot, mat, 0), src[0:1024, :].rearrange("(kc p) f -> p kc f", p=128),
                     s_wgwu[slot], w)
                tok = gdma(wgu_dst(slot, mat, 8),
                           src[1024:2048, :].rearrange("(kc p) f -> p kc f", p=128),
                           s_wgwu[slot], w)
            return tok

        ln2_tails = []
        ln2_state = {"x1_free": None, "y2_free": [None, None]}

        def make_ln2_closures(tile):
            def tail(tg8, tg):
                y2p = y2x[:, :]
                tok_s1 = dve(lambda v, o=stat[:, c0:c0 + 1], i=y2p:
                             v.reduce_sum(o, i, AX.X), [], inc=True)
                tok_ssq = dve(lambda v, o=x1row[:, :], a=y2p:
                              v.tensor_tensor(o, a, a, ALU.mult),
                              [ln2_state["x1_free"]], inc=True)
                tok_s2 = dve(lambda v, o=stat[:, c0 + 1:c0 + 2], i=x1row[:, :]:
                             v.reduce_sum(o, i, AX.X), [(s_dve, tok_ssq)], inc=True)
                tok_m = dve(lambda v, o=stat[:, c0 + 2:c0 + 3], i=stat[:, c0:c0 + 1]:
                            v.tensor_scalar(o, i, 1.0 / D, None, ALU.mult),
                            [(s_dve, tok_s1)], inc=True)
                tok_e = dve(lambda v, o=stat[:, c0 + 3:c0 + 4], i=stat[:, c0 + 1:c0 + 2]:
                            v.tensor_scalar(o, i, 1.0 / D, None, ALU.mult),
                            [(s_dve, tok_s2)], inc=True)
                tok_mq = dve(lambda v, o=stat[:, c0 + 4:c0 + 5], a=stat[:, c0 + 2:c0 + 3]:
                             v.tensor_tensor(o, a, a, ALU.mult), [(s_dve, tok_m)], inc=True)
                tok_var = dve(lambda v, o=stat[:, c0 + 3:c0 + 4], a=stat[:, c0 + 3:c0 + 4],
                              b=stat[:, c0 + 4:c0 + 5]: v.tensor_tensor(o, a, b, ALU.subtract),
                              [(s_dve, tok_e), (s_dve, tok_mq)], inc=True)
                act(lambda a, o=stat[:, c0 + 4:c0 + 5], i=stat[:, c0 + 3:c0 + 4],
                    b=kcol[:, 1:2]: a.activation(o, i, AF.Ln, bias=b), [(s_dve, tok_var)])
                tok_rstd = act(lambda a, o=stat[:, c0 + 5:c0 + 6], i=stat[:, c0 + 4:c0 + 5]:
                               a.activation(o, i, AF.Exp, scale=-0.5), [], inc=True)
                tok_nmr = dve(lambda v, o=stat[:, c0 + 6:c0 + 7], i=stat[:, c0 + 2:c0 + 3],
                              r=stat[:, c0 + 5:c0 + 6]:
                              v.scalar_tensor_tensor(o, i, -1.0, r, ALU.mult, ALU.mult),
                              [(s_act, tok_rstd), (s_dve, tok_m)], inc=True)
                tok_o = dve(lambda v, o=x1row[:, :], i=y2p, r=stat[:, c0 + 5:c0 + 6],
                            n=stat[:, c0 + 6:c0 + 7]:
                            v.tensor_scalar(o, i, r, n, ALU.mult, ALU.add),
                            [(s_dve, tok_nmr)], inc=True)
                tok_st = sdma(out_d[tg * 128:(tg + 1) * 128, :], x1row[:, :], s_out,
                              [(s_dve, tok_o)])
                ln2_state["x1_free"] = (s_out, tok_st)
                ln2_state["y2_free"][0] = (s_dve, tok_o)

            def load_add(tg8, tg):
                acc_row = out_acc[:, tg8 * D:(tg8 + 1) * D]
                y2p = y2x[:, :]
                w = list(x1w_all) + list(gateL)
                if ln2_state["x1_free"]:
                    w.append(ln2_state["x1_free"])
                tok_ld = sdma(x1row[:, :], x1f_d[tg * 128:(tg + 1) * 128, :], s_x1r, w)
                yfree = ln2_state["y2_free"][0]
                dve(lambda v, o=y2p, a=acc_row, b=x1row[:, :]:
                    v.tensor_tensor(o, a, b, ALU.add),
                    [(s_x1r, tok_ld)] + ([yfree] if yfree else []), inc=True)

            def closure(k):
                def run():
                    if k >= 1:
                        tail(k - 1, tile * 8 + k - 1)
                    if k < 8:
                        load_add(k, tile * 8 + k)
                return run
            return [closure(k) for k in range(9)]

        for tile in range(NTILE):
            t0 = tile * TT
            for e in range(E):
                for half in range(NHALF):
                    slab_tok = [load_wgwu(e, half, 0, 0), load_wgwu(e, half, 1, 1)]
                    next_sidx = 2
                    tok_h = None
                    for sidx in range(4):
                        slot = sidx % 2
                        sw = [(s_wgwu[slot], slab_tok[slot])]
                        for fb in range(2):
                            fc_local = sidx * 2 + fb
                            for ts in range(2):
                                gp = gu_group % 2
                                gu_group += 1
                                rhs = [xTv[:, kc, t0 + ts * 512: t0 + (ts + 1) * 512]
                                       for kc in range(KC)]
                                gpairs = [(wgu_tile(slot, 0, kc, fb), rhs[kc]) for kc in range(KC)]
                                upairs = [(wgu_tile(slot, 1, kc, fb), rhs[kc]) for kc in range(KC)]
                                ew = sw + x1T_w
                                if tile == 0 and e == 0 and half == 0:
                                    ew = ew + gateC
                                tok_g = mm_group(ps[gp][:, :], gpairs, ew, bank=f"ps{gp}")
                                tok_u = mm_group(ps[2 + gp][:, :], upairs, [], bank=f"ps{2 + gp}")
                                wgwu_lastuse[slot] = (s_pe, tok_u)
                                hsl = h_cols(fc_local, ts * 512, 512)
                                hw = [h_lastuse] if h_lastuse else list(gateL)
                                tok_si = act(lambda a, o=hsl, i=ps[gp][:, :]:
                                             a.activation(o, i, AF.Silu),
                                             [(s_pe, tok_g)] + hw, inc=True)
                                bank_free[f"ps{gp}"] = (s_act, tok_si)
                                tok_h = dve(lambda v, o=hsl, i=ps[2 + gp][:, :]:
                                            v.tensor_tensor(o, o, i, ALU.mult),
                                            [(s_pe, tok_u), (s_act, tok_si)], inc=True)
                                bank_free[f"ps{2 + gp}"] = (s_dve, tok_h)
                        if next_sidx < 4:
                            slab_tok[next_sidx % 2] = load_wgwu(e, half, next_sidx, next_sidx % 2)
                            next_sidx += 1
                        if ln2_tails:
                            ln2_tails.pop(0)()
                        if ln2_tails:
                            ln2_tails.pop(0)()
                    h_ready = (s_dve, tok_h)
                    tok_d = None
                    for dt in range(4):
                        sl0 = wd_slot_ctr[0] % 3
                        sl1 = (wd_slot_ctr[0] + 1) % 3
                        wd_slot_ctr[0] += 2
                        toks_wd = []
                        for fch, slot in ((0, sl0), (1, sl1)):
                            w = list(gateC) + ([wd_lastuse[slot]] if wd_lastuse[slot] else [])
                            src = wd_d[e * D + half * 1024 + fch * 512:
                                       e * D + half * 1024 + (fch + 1) * 512,
                                       dt * 512:(dt + 1) * 512]
                            toks_wd.append((s_wd[slot],
                                            gdma(wd_dst(slot),
                                                 src.rearrange("(fc p) d -> p fc d", p=128),
                                                 s_wd[slot], w)))
                        for tg8 in range(8):
                            dp = d_group % 2
                            d_group += 1
                            pairs = [(h_cols(fc, tg8 * 128, 128),
                                      wd_tile(sl0 if fc < 4 else sl1, fc % 4))
                                     for fc in range(8)]
                            tok_d = mm_group(ps[4 + dp][:, :], pairs,
                                             [h_ready] + toks_wd, bank=f"ps{4 + dp}")
                            wd_lastuse[sl0] = (s_pe, tok_d)
                            wd_lastuse[sl1] = (s_pe, tok_d)
                            acc_sl = out_acc[:, tg8 * D + dt * 512: tg8 * D + (dt + 1) * 512]
                            wcol = rtr[:, 160 + (tile * 8 + tg8) * 8 + e:
                                       161 + (tile * 8 + tg8) * 8 + e]
                            ew2 = [(s_pe, tok_d)] + wtok_w
                            if not acc_started:
                                ew2 = ew2 + accgate
                            if e == 0 and half == 0:
                                tok_ac = dve(lambda v, o=acc_sl, i=ps[4 + dp][:, :], m=wcol:
                                             v.tensor_scalar(o, i, m, None, ALU.mult),
                                             ew2, inc=True)
                            else:
                                tok_ac = dve(lambda v, o=acc_sl, i=ps[4 + dp][:, :], m=wcol:
                                             v.scalar_tensor_tensor(o, i, m, o, ALU.mult, ALU.add),
                                             ew2, inc=True)
                            acc_started = True
                            bank_free[f"ps{4 + dp}"] = (s_dve, tok_ac)
                    h_lastuse = (s_pe, tok_d)
            ln2_tails.extend(make_ln2_closures(tile))

        while ln2_tails:
            ln2_tails.pop(0)()

        S.op("sync", lambda s, n=s_out.n: s.wait_ge(s_out.h, n), (), None)
        S.check()
        _emit_block(nc, S)
    return nc


def _emit_block(nc, S):
    with nc.Block() as block:
        @block.tensor
        def _(t):
            S.replay("tensor", t)

        @block.vector
        def _(v):
            S.replay("vector", v)

        @block.scalar
        def _(sc):
            S.replay("scalar", sc)

        @block.gpsimd
        def _(g):
            S.replay("gpsimd", g)

        @block.sync
        def _(sy):
            S.replay("sync", sy)


def kernel(L0_attn_hidden_states, L0_attn_wq, L0_attn_wk, L0_attn_wv, L0_attn_wo,
           L0_ffn_Wg, L0_ffn_Wu, L0_ffn_Wd, L0_ffn_Wr, L0_ffn_temperature):
    x = np.ascontiguousarray(np.asarray(L0_attn_hidden_states, dtype=np.float32))
    base = {
        "wq": np.ascontiguousarray(np.asarray(L0_attn_wq, dtype=np.float32)),
        "wk": np.ascontiguousarray(np.asarray(L0_attn_wk, dtype=np.float32)),
        "wv": np.ascontiguousarray(np.asarray(L0_attn_wv, dtype=np.float32)),
        "wo": np.ascontiguousarray(np.asarray(L0_attn_wo, dtype=np.float32)),
        "wg3": np.ascontiguousarray(np.asarray(L0_ffn_Wg, dtype=np.float32).reshape(E * D, D)),
        "wu3": np.ascontiguousarray(np.asarray(L0_ffn_Wu, dtype=np.float32).reshape(E * D, D)),
        "wd3": np.ascontiguousarray(np.asarray(L0_ffn_Wd, dtype=np.float32).reshape(E * D, D)),
        "wr": np.ascontiguousarray(np.asarray(L0_ffn_Wr, dtype=np.float32)),
        "tempcol": np.broadcast_to(
            np.asarray(L0_ffn_temperature, dtype=np.float32).reshape(1, 1), (128, 1)).copy(),
    }
    cos2, sin2, sgn, ident, maskt = _host_consts()
    kcol = np.zeros((128, 4), np.float32)
    kcol[:, 0] = -GOLDEN_CENTER
    kcol[:, 1] = LN_EPS
    base.update({"cos2": np.asarray(cos2), "sin2": np.asarray(sin2), "sgncol": sgn,
                 "identb": np.asarray(ident), "maskt": np.asarray(maskt), "kcol": kcol})
    nc = build_nc()
    in_maps = [dict(base, x0=np.ascontiguousarray(x[b])) for b in range(NCORE)]
    res = run_bass_kernel_spmd(nc, in_maps, core_ids=list(range(NCORE)))
    return np.stack([res.results[b]["out"] for b in range(NCORE)], axis=0)


# version 20 (final)
# speedup vs baseline: 1.1400x; 1.1008x over previous
import numpy as np
import ml_dtypes
from contextlib import ExitStack

import concourse.bass as bass
import concourse.mybir as mybir
from concourse.bass_utils import run_bass_kernel_spmd

F32 = mybir.dt.float32
BF16 = mybir.dt.bfloat16
AF = mybir.ActivationFunctionType
ALU = mybir.AluOpType
AX = mybir.AxisListType

D = 2048
T = 2048
KC = 16
H = 16
HD = 128
E = 8
TG = 16
NCORE = 8
NEG = -1.0e30
QSCALE = float(np.float32(HD ** -0.5))
GOLDEN_CENTER = float(np.float32(1.0 / np.e))
GOLDEN_LOWER = float(np.float32(0.5 - np.log(4.0 / 3.0)))
GOLDEN_UPPER = 0.5
LN_EPS = 1e-5

NTILE = 2
TT = 1024
NHALF = 2
FT = 256
DT = 512


def _host_consts():
    inv_freq = (1.0 / (10000.0 ** (np.arange(0, HD, 2, dtype=np.float64) / HD))).astype(np.float32)
    pos = np.arange(T, dtype=np.float32)
    freqs = pos[None, :] * inv_freq[:, None]
    cos = np.cos(freqs).astype(np.float32)
    sin = np.sin(freqs).astype(np.float32)
    cos2 = np.concatenate([cos, cos], 0).astype(ml_dtypes.bfloat16)
    sin2 = np.concatenate([sin, sin], 0).astype(ml_dtypes.bfloat16)
    sgn = np.concatenate([np.full((64, 1), -1.0, np.float32),
                          np.full((64, 1), 1.0, np.float32)], 0)
    ident = np.eye(128, dtype=np.float32).astype(ml_dtypes.bfloat16)
    r = np.arange(128)[:, None]
    c = np.arange(128)[None, :]
    blocks = []
    for u in range(15):
        o = 11 - u
        if o == 8:
            m = np.where(c > r, 0.0, NEG)
        elif o == 0:
            m = np.where(c <= r, 0.0, NEG)
        elif 1 <= o <= 7:
            m = np.zeros((128, 128))
        else:
            m = np.full((128, 128), NEG)
        blocks.append(m)
    maskt = np.concatenate(blocks, 1).astype(np.float32).astype(ml_dtypes.bfloat16)
    return cos2, sin2, sgn, ident, maskt


class Sem:
    __slots__ = ("name", "n", "h", "max_waited")

    def __init__(self, name):
        self.name = name
        self.n = 0
        self.h = None
        self.max_waited = 0


class Sched:
    ENGINES = ("tensor", "vector", "scalar", "gpsimd", "sync")

    def __init__(self):
        self.q = {e: [] for e in self.ENGINES}
        self._lastw = {e: {} for e in self.ENGINES}
        self.sems = {}

    def sem(self, name):
        s = Sem(name)
        self.sems[name] = s
        return s

    def op(self, eng, fn, waits=(), inc=None, inck=1):
        wl = []
        for item in waits:
            if item is None:
                continue
            if len(item) == 3:
                s, tok, force = item
            else:
                s, tok = item
                force = False
            if s is None or tok is None or tok <= 0:
                continue
            lw = self._lastw[eng].get(s.name, 0)
            if force or tok > lw:
                tok2 = max(tok, lw)
                wl.append((s, tok2))
                self._lastw[eng][s.name] = tok2
                s.max_waited = max(s.max_waited, tok2)
        tok_out = None
        if inc is not None:
            inc.n += inck
            tok_out = inc.n
            self.q[eng].append((fn, wl, (inc, inck)))
        else:
            self.q[eng].append((fn, wl, None))
        return tok_out

    def check(self):
        for s in self.sems.values():
            assert s.max_waited <= s.n, f"sem {s.name}: waits {s.max_waited} > incs {s.n} (hang)"

    def replay(self, eng_name, eng):
        for fn, wl, inc in self.q[eng_name]:
            for s, tok in wl:
                eng.wait_ge(s.h, tok)
            instr = fn(eng)
            if inc is not None:
                s, k = inc
                instr.then_inc(s.h, k)


def build_nc(taps=(), upto="all"):
    nc = bass.Bass("TRN2", target_bir_lowering=False)
    S = Sched()

    x0_d = nc.dram_tensor("x0", [T, D], F32, kind="ExternalInput").ap()
    wq_d = nc.dram_tensor("wq", [D, D], F32, kind="ExternalInput").ap()
    wk_d = nc.dram_tensor("wk", [D, D], F32, kind="ExternalInput").ap()
    wv_d = nc.dram_tensor("wv", [D, D], F32, kind="ExternalInput").ap()
    wo_d = nc.dram_tensor("wo", [D, D], F32, kind="ExternalInput").ap()
    wg_d = nc.dram_tensor("wg3", [E * D, D], F32, kind="ExternalInput").ap()
    wu_d = nc.dram_tensor("wu3", [E * D, D], F32, kind="ExternalInput").ap()
    wd_d = nc.dram_tensor("wd3", [E * D, D], F32, kind="ExternalInput").ap()
    wr_d = nc.dram_tensor("wr", [D, E], F32, kind="ExternalInput").ap()
    temp_d = nc.dram_tensor("tempcol", [128, 1], F32, kind="ExternalInput").ap()
    cos_d = nc.dram_tensor("cos2", [128, T], BF16, kind="ExternalInput").ap()
    sin_d = nc.dram_tensor("sin2", [128, T], BF16, kind="ExternalInput").ap()
    sgn_d = nc.dram_tensor("sgncol", [128, 1], F32, kind="ExternalInput").ap()
    id_d = nc.dram_tensor("identb", [128, 128], BF16, kind="ExternalInput").ap()
    msk_d = nc.dram_tensor("maskt", [128, 1920], BF16, kind="ExternalInput").ap()
    kcol_d = nc.dram_tensor("kcol", [128, 4], F32, kind="ExternalInput").ap()
    out_d = nc.dram_tensor("out", [T, D], F32, kind="ExternalOutput").ap()
    ybuf_d = nc.dram_tensor("ybuf", [T, D], F32, kind="Internal").ap()
    x1f_d = nc.dram_tensor("x1f", [T, D], F32, kind="Internal").ap()
    tap_d = {}
    if "x1t" in taps:
        tap_d["x1t"] = nc.dram_tensor("tap_x1t", [128, KC * T], BF16, kind="ExternalOutput").ap()
    if "outh" in taps:
        tap_d["outh"] = nc.dram_tensor("tap_outh", [128, H * T], BF16, kind="ExternalOutput").ap()
    if "router" in taps:
        tap_d["router"] = nc.dram_tensor("tap_router", [128, 128], F32, kind="ExternalOutput").ap()
    if "x1" in taps:
        tap_d["x1"] = nc.dram_tensor("tap_x1", [T, D], F32, kind="ExternalOutput").ap()
    if "y" in taps:
        tap_d["y"] = nc.dram_tensor("tap_y", [T, D], F32, kind="ExternalOutput").ap()
    if "x0t" in taps:
        tap_d["x0t"] = nc.dram_tensor("tap_x0t", [128, KC * T], BF16, kind="ExternalOutput").ap()
    if "qkv" in taps:
        tap_d["qt"] = nc.dram_tensor("tap_qt", [128, T], BF16, kind="ExternalOutput").ap()
        tap_d["kt"] = nc.dram_tensor("tap_kt", [128, T], BF16, kind="ExternalOutput").ap()
        tap_d["vt"] = nc.dram_tensor("tap_vt", [128, T], BF16, kind="ExternalOutput").ap()
    if "probs" in taps:
        tap_d["probs"] = nc.dram_tensor("tap_probs", [128, 9216], BF16, kind="ExternalOutput").ap()
        tap_d["probst"] = nc.dram_tensor("tap_probst", [128, 6144], BF16, kind="ExternalOutput").ap()
        tap_d["stat"] = nc.dram_tensor("tap_stat", [128, 32], F32, kind="ExternalOutput").ap()

    s_pe = S.sem("pe"); s_act = S.sem("act"); s_dve = S.sem("dve")
    s_bulk = S.sem("bulk")
    s_gbulk = S.sem("gbulk")
    s_pool = S.sem("pool")
    s_wsl = [S.sem(f"wsl{i}") for i in range(4)]
    s_wgwu = [S.sem(f"wgwu{i}") for i in range(2)]
    s_wd = [S.sem(f"wd{i}") for i in range(2)]
    s_wo = S.sem("wo")
    s_x0s = [S.sem(f"x0s{i}") for i in range(4)]
    s_yw = S.sem("yw")
    s_yr = [S.sem(f"yr{i}") for i in range(2)]
    s_x1w = S.sem("x1w")
    s_x1r = S.sem("x1r")
    s_out = S.sem("sout")

    def pe(fn, waits=(), inc=False):
        return S.op("tensor", fn, waits, s_pe if inc else None)

    def dve(fn, waits=(), inc=False):
        w = list(waits) + [(s_dve, s_dve.n, True)]
        return S.op("vector", fn, w, s_dve)

    def act(fn, waits=(), inc=False):
        w = list(waits) + [(s_act, s_act.n, True)]
        return S.op("scalar", fn, w, s_act)

    def pool(fn, waits=()):
        w = list(waits) + [(s_pool, s_pool.n, True)]
        return S.op("gpsimd", fn, w, s_pool)

    last_dma = {}

    def _dma(eng, dst, src, sem, waits, tkey):
        w = list(waits)
        if tkey is None:
            tkey = dst.tensor.name if hasattr(dst, "tensor") else None
        if tkey is not None and tkey in last_dma:
            ps = last_dma[tkey]
            w.append((ps, ps.n, True))
        tok = S.op(eng, (lambda g, d=dst, s=src: g.dma_start(d, s)), w, sem, 16)
        if tkey is not None:
            last_dma[tkey] = sem
        return tok

    def gdma(dst, src, sem, waits=(), tkey=None):
        return _dma("gpsimd", dst, src, sem, waits, tkey)

    def sdma(dst, src, sem, waits=(), tkey=None):
        return _dma("sync", dst, src, sem, waits, tkey)

    with ExitStack() as ctx:
        for s in S.sems.values():
            s.h = ctx.enter_context(nc.semaphore(s.name))

        xT = ctx.enter_context(nc.sbuf_tensor("xT", [128, KC * T], BF16))
        identb = ctx.enter_context(nc.sbuf_tensor("identb_s", [128, 128], BF16))
        wrb = ctx.enter_context(nc.sbuf_tensor("wrb", [128, KC * E], BF16))
        rtr = ctx.enter_context(nc.sbuf_tensor("rtr", [128, 288], F32))
        stat = ctx.enter_context(nc.sbuf_tensor("stat", [128, 64], F32))
        sgncol = ctx.enter_context(nc.sbuf_tensor("sgncol_s", [128, 1], F32))
        tempcol = ctx.enter_context(nc.sbuf_tensor("tempcol_s", [128, 1], F32))
        kcol = ctx.enter_context(nc.sbuf_tensor("kcol_s", [128, 4], F32))
        ctxA = ExitStack()
        ctx.callback(ctxA.close)
        big = ctxA.enter_context(nc.sbuf_tensor("big", [128, H * T], BF16))

        psAB = [ctx.enter_context(nc.psum_tensor(n, [128, 1536], F32)) for n in ("psA", "psB")]

        def psx(par, reg):
            return psAB[par][:, reg * 512:(reg + 1) * 512]

        ps = [psx(0, 0), psx(1, 0), psx(0, 1), psx(1, 1), psx(0, 2), psx(1, 2)]
        ptr = ctx.enter_context(nc.psum_tensor("ptr", [128, 1024], BF16))
        psav = ctx.enter_context(nc.psum_tensor("psav", [128, 512], F32))

        bank_free = {}

        def mm_group(psum_ap, pairs, extra_waits=(), bank=None):
            n = len(pairs)
            tok = None
            for i, (l, r) in enumerate(pairs):
                w = list(extra_waits) if i == 0 else []
                if i == 0 and bank is not None and bank in bank_free:
                    bf = bank_free[bank]
                    w.extend(bf if isinstance(bf, list) else [bf])
                tok = S.op("tensor",
                           (lambda t, o=psum_ap, a=l, b=r, st=(i == 0), sp=(i == n - 1):
                            t.matmul(o, a, b, start=st, stop=sp)),
                           w, s_pe if i == n - 1 else None)
            return tok

        sdma(identb[:, :], id_d[:, :], s_bulk)
        sdma(sgncol[:, :], sgn_d[:, :], s_bulk)
        sdma(tempcol[:, :], temp_d[:, :], s_bulk)
        bigv = big.ap().rearrange("p (tg d) -> p tg d", d=D)
        x0v = x0_d.rearrange("(tg p) d -> p tg d", p=128)
        gdma(bigv[:, 0:8], x0v[:, 0:8], s_gbulk)
        gdma(bigv[:, 8:16], x0v[:, 8:16], s_gbulk)
        gdma(wrb.ap().rearrange("p (kc e) -> p kc e", e=E),
             wr_d.rearrange("(kc p) e -> p kc e", p=128), s_gbulk)
        sdma(kcol[:, :], kcol_d[:, :], s_bulk)

        ctxH = ExitStack()
        ctx.callback(ctxH.close)
        cos2 = ctxH.enter_context(nc.sbuf_tensor("cos2_s", [128, T], BF16))
        sin2 = ctxH.enter_context(nc.sbuf_tensor("sin2_s", [128, T], BF16))
        maskt = ctxH.enter_context(nc.sbuf_tensor("maskt_s", [128, 1920], BF16))
        qT = ctxH.enter_context(nc.sbuf_tensor("qT", [128, T], BF16))
        kT = ctxH.enter_context(nc.sbuf_tensor("kT", [128, T], BF16))
        vT = ctxH.enter_context(nc.sbuf_tensor("vT", [128, T], BF16))
        probs = ctxH.enter_context(nc.sbuf_tensor("probs", [128, 9216], BF16))
        probsT = ctxH.enter_context(nc.sbuf_tensor("probsT", [128, 6144], BF16))
        ropet = ctxH.enter_context(nc.sbuf_tensor("ropet", [128, 1024], BF16))
        zerob = ctxH.enter_context(nc.sbuf_tensor("zerob", [128, 128], BF16))
        wsl = [ctxH.enter_context(nc.sbuf_tensor(f"wsl{i}", [128, KC * 128], BF16))
               for i in range(4)]

        sdma(cos2[:, :], cos_d[:, :], s_bulk)
        sdma(sin2[:, :], sin_d[:, :], s_bulk)
        sdma(maskt[:, :], msk_d[:, :], s_bulk)
        tok_z = dve(lambda v, o=zerob[:, :]: v.memset(o, 0.0), [], inc=True)
        bulk_w = [(s_bulk, s_bulk.n), (s_gbulk, s_gbulk.n), (s_dve, tok_z)]

        xTv = xT.ap().rearrange("p (kc t) -> p kc t", t=T)
        ptrv = ptr.ap().rearrange("p (b t) -> p b t", t=128)
        ptr_free = None
        x0T_done = None
        for tg in range(TG):
            for g8 in range(2):
                tok_tr = None
                for j in range(8):
                    dc = g8 * 8 + j
                    w = list(bulk_w)
                    if j == 0 and ptr_free is not None:
                        w.append(ptr_free)
                    tok_tr = pe(lambda t, o=ptr[:, j * 128:(j + 1) * 128],
                                i=bigv[:, tg, dc * 128:(dc + 1) * 128], idn=identb[:, :]:
                                t.transpose(o, i, idn), w, inc=(j == 7))
                dst = xTv[:, g8 * 8:(g8 + 1) * 8, tg * 128:(tg + 1) * 128]
                x0T_done = dve(lambda v, d=dst, s=ptrv: v.tensor_copy(d, s),
                               [(s_pe, tok_tr)], inc=True)
                ptr_free = (s_dve, x0T_done)
        x0T_w = [(s_dve, x0T_done)]
        tok_last_x0big_pe = s_pe.n

        if "x0t" in tap_d:
            sdma(tap_d["x0t"][:, :], xT[:, :], s_out, x0T_w)

        wslv = [w.ap().rearrange("p (kc m) -> p kc m", m=128) for w in wsl]
        slot_ready = [None] * 4
        slot_lastuse = [None] * 4
        nslice = H * 3

        def issue_slice_load(j):
            hh, m = divmod(j, 3)
            mat = (wq_d, wk_d, wv_d)[m]
            slot = j % 4
            free_w = [slot_lastuse[slot]] if slot_lastuse[slot] else []
            src = mat[:, hh * 128:(hh + 1) * 128]
            free_f = [(w[0], w[1], True) for w in free_w]
            gdma(wslv[slot][:, 0:8], src[0:1024, :].rearrange("(kc p) m -> p kc m", p=128),
                 s_wsl[slot], free_f)
            tok = gdma(wslv[slot][:, 8:16], src[1024:2048, :].rearrange("(kc p) m -> p kc m", p=128),
                       s_wsl[slot], free_f)
            slot_ready[slot] = (s_wsl[slot], tok)

        for j in range(min(4, nslice)):
            issue_slice_load(j)

        qkT_free = None
        vT_free = None
        probs_free = None
        probsT_free = None
        psav_free = None
        rope_prev = None
        outh_tok = None
        qT_ready = kT_ready = None

        maskF = maskt[:, 3 * 128:4 * 128]
        maskD = maskt[:, 11 * 128:12 * 128]
        probs_free = [None] * 8
        stat_free = [None] * 8
        pt_last_act = None
        pt_memset_tok = None

        for h in range(H):
            for m, (arena, scale) in enumerate(((qT, QSCALE), (kT, 1.0))):
                slot = (h * 3 + m) % 4
                rw = [slot_ready[slot]]
                tok_rope = None
                for tb in range(4):
                    bank = f"ps{tb % 2}"
                    pairs = [(wslv[slot][:, kc, :], xTv[:, kc, tb * 512:(tb + 1) * 512])
                             for kc in range(KC)]
                    tok_mm = mm_group(ps[tb % 2][:, :], pairs, rw + x0T_w, bank=bank)
                    slot_lastuse[slot] = (s_pe, tok_mm)
                    tcols = slice(tb * 512, (tb + 1) * 512)
                    q_raw = ropet[:, 0:512]
                    t_sw = ropet[:, 512:1024]
                    wv_ev = [(s_pe, tok_mm)]
                    if rope_prev is not None:
                        wv_ev.append(rope_prev)
                    tok_ev = act(lambda a, o=q_raw, i=ps[tb % 2][:, :], sc=scale:
                                 a.activation(o, i, AF.Copy, scale=sc), wv_ev, inc=True)
                    bank_free[bank] = [(s_act, tok_ev)]
                    dve(lambda v, o=t_sw[0:64, :], i=q_raw[64:128, :]: v.tensor_copy(o, i),
                        [(s_act, tok_ev)] + ([rope_prev] if rope_prev else []))
                    tok_sw = dve(lambda v, o=t_sw[64:128, :], i=q_raw[0:64, :]:
                                 v.tensor_copy(o, i), [], inc=True)
                    dve(lambda v, o=t_sw, i=sin2[:, tcols]:
                        v.tensor_tensor(o, o, i, ALU.mult), bulk_w)
                    dve(lambda v, o=q_raw, i=cos2[:, tcols]: v.tensor_tensor(o, o, i, ALU.mult),
                        [])
                    ws = [qkT_free] if qkT_free else []
                    tok_rope = dve(lambda v, o=arena[:, tcols], i0=t_sw, s=sgncol[:, :], i1=q_raw:
                                   v.scalar_tensor_tensor(o, i0, s, i1, ALU.mult, ALU.add),
                                   ws + bulk_w, inc=True)
                    rope_prev = (s_dve, tok_rope)
                if m == 0:
                    qT_ready = (s_dve, tok_rope)
                else:
                    kT_ready = (s_dve, tok_rope)
            slot = (h * 3 + 2) % 4
            rw = [slot_ready[slot]]
            vtok_ev = None
            for tb in range(4):
                bank = f"ps{tb % 2}"
                pairs = [(wslv[slot][:, kc, :], xTv[:, kc, tb * 512:(tb + 1) * 512])
                         for kc in range(KC)]
                tok_mm = mm_group(ps[tb % 2][:, :], pairs, rw + x0T_w, bank=bank)
                slot_lastuse[slot] = (s_pe, tok_mm)
                wv_ev = [(s_pe, tok_mm)]
                if vT_free is not None:
                    wv_ev.append(vT_free)
                vtok_ev = act(lambda a, o=vT[:, tb * 512:(tb + 1) * 512], i=ps[tb % 2][:, :]:
                              a.activation(o, i, AF.Copy), wv_ev, inc=True)
                bank_free[bank] = [(s_act, vtok_ev)]
            vT_feat = (s_act, vtok_ev)
            for jn in range(h * 3 + 3, min(h * 3 + 6, nslice)):
                issue_slice_load(jn)
            tok_vtok = None
            for g8 in range(2):
                tok_tr = None
                for jj in range(8):
                    i = g8 * 8 + jj
                    w = [vT_feat]
                    if jj == 0 and ptr_free is not None:
                        w.append(ptr_free)
                    tok_tr = pe(lambda t, o=ptr[:, jj * 128:(jj + 1) * 128],
                                i_=vT[:, i * 128:(i + 1) * 128], idn=identb[:, :]:
                                t.transpose(o, i_, idn), w, inc=(jj == 7))
                dst = vT.ap().rearrange("p (b t) -> p b t", t=128)[:, g8 * 8:(g8 + 1) * 8, :]
                tok_vtok = act(lambda a, d=dst, s=ptrv: a.activation(d, s, AF.Copy),
                               [(s_pe, tok_tr)], inc=True)
                ptr_free = (s_act, tok_vtok)
            v_tok_ready = (s_act, tok_vtok)

            SKEW = 6
            units = [(qt, qi) for qt in range(4) for qi in range(4)]
            uinfo = [None] * 16

            def p1(u):
                nonlocal qkT_free
                qt, qi = units[u]
                qg = qt * 4 + qi
                w0b = max(0, qg - 8)
                nwb = qg + 1 - w0b
                wcols = nwb * 128
                par = u % 2
                pspar = psAB[par]
                pb = u % 8
                prb = probs[:, pb * 1152: pb * 1152 + wcols]
                st = pb * 4
                bw = list(stat_free[pb] or [])
                for reg in range(3):
                    k = f"ps{reg * 2 + par}"
                    if k in bank_free:
                        bf = bank_free[k]
                        bw += bf if isinstance(bf, list) else [bf]
                off = 0
                first = True
                tok_s = None
                while off < wcols:
                    cw = min(512, wcols - off)
                    tok_s = S.op("tensor",
                                 (lambda t, o=pspar[:, off:off + cw],
                                  a=qT[:, qg * 128:(qg + 1) * 128],
                                  b=kT[:, w0b * 128 + off: w0b * 128 + off + cw]:
                                  t.matmul(o, a, b, start=True, stop=True)),
                                 ([qT_ready, kT_ready] + bw) if first else [], s_pe)
                    first = False
                    off += cw
                qkT_free = (s_pe, tok_s)
                if qg >= 8:
                    S.op("tensor", (lambda t, o=pspar[:, 0:128], a=identb[:, :], b=maskF:
                                    t.matmul(o, a, b, start=False, stop=True)), bulk_w, None)
                tok_s = S.op("tensor",
                             (lambda t, o=pspar[:, wcols - 128:wcols], a=identb[:, :], b=maskD:
                              t.matmul(o, a, b, start=False, stop=True)), bulk_w, s_pe)
                tok_max = dve(lambda v, o=stat[:, st:st + 1], i=pspar[:, 0:wcols]:
                              v.reduce_max(o, i, AX.X, negate=True), [(s_pe, tok_s)], inc=True)
                pw = [(s_dve, tok_max)]
                if probs_free[pb]:
                    pw.append(probs_free[pb])
                tok_exp = act(lambda a, o=prb, i=pspar[:, 0:wcols], b=stat[:, st:st + 1]:
                              a.activation(o, i, AF.Exp, bias=b), pw, inc=True)
                stat_free[pb] = [(s_act, tok_exp)]
                for reg in range(3):
                    bank_free[f"ps{reg * 2 + par}"] = [(s_act, tok_exp)]
                tok_rs = dve(lambda v, o=stat[:, st + 1:st + 2], i=prb:
                             v.reduce_sum(o, i, AX.X), [(s_act, tok_exp)], inc=True)
                tok_rc = dve(lambda v, o=stat[:, st + 2:st + 3], i=stat[:, st + 1:st + 2]:
                             v.reciprocal(o, i), [(s_dve, tok_rs)], inc=True)
                tok_nrm = dve(lambda v, o=prb, s=stat[:, st + 2:st + 3]:
                              v.tensor_scalar(o, o, s, None, ALU.mult),
                              [(s_dve, tok_rc)], inc=True)
                uinfo[u] = (qt, qi, qg, w0b, nwb, pb, tok_nrm)

            def p2(u):
                nonlocal ptr_free, pt_last_act, pt_memset_tok
                qt, qi, qg, w0b, nwb, pb, tok_nrm = uinfo[u]
                s0b = max(0, qt * 4 - 8)
                wb = qt * 4 + 4 - s0b
                prbase = pb * 1152
                pfw = [probsT_free] if probsT_free else []
                for p in range(wb):
                    blk = s0b + p
                    if blk < w0b or blk > qg:
                        pt_memset_tok = dve(
                            lambda v, o=probsT[:, p * 512 + qi * 128: p * 512 + (qi + 1) * 128]:
                            v.memset(o, 0.0), pfw, inc=True)
                for b0 in range(0, nwb, 8):
                    bn = min(8, nwb - b0)
                    tok_tr = None
                    for jj in range(bn):
                        w = [(s_dve, tok_nrm)]
                        if jj == 0 and ptr_free is not None:
                            w.append(ptr_free)
                        tok_tr = pe(lambda t, o=ptr[:, jj * 128:(jj + 1) * 128],
                                    i_=probs[:, prbase + (b0 + jj) * 128:
                                             prbase + (b0 + jj + 1) * 128],
                                    idn=identb[:, :]:
                                    t.transpose(o, i_, idn), w, inc=(jj == bn - 1))
                    p0 = w0b - s0b + b0
                    dst = probsT.ap().rearrange("p (b q) -> p b q", q=512)[
                        :, p0:p0 + bn, qi * 128:(qi + 1) * 128]
                    w2 = [(s_pe, tok_tr)] + pfw
                    tok_pt = act(lambda a, d=dst, s=ptrv[:, 0:bn, :]:
                                 a.activation(d, s, AF.Copy), w2, inc=True)
                    ptr_free = (s_act, tok_pt)
                    pt_last_act = tok_pt
                probs_free[pb] = (s_pe, tok_tr)

            def do_av(qt):
                nonlocal vT_free, probsT_free, psav_free, outh_tok
                s0b = max(0, qt * 4 - 8)
                wb = qt * 4 + 4 - s0b
                pairs = [(vT[:, (s0b + p) * 128:(s0b + p + 1) * 128],
                          probsT[:, p * 512:(p + 1) * 512]) for p in range(wb)]
                ew = [(s_act, pt_last_act), v_tok_ready]
                if pt_memset_tok is not None:
                    ew.append((s_dve, pt_memset_tok))
                if psav_free is not None:
                    ew.append(psav_free)
                tok_av = mm_group(psav[:, :], pairs, ew, bank=None)
                vT_free = (s_pe, tok_av)
                probsT_free = (s_pe, tok_av)
                w3 = [(s_pe, tok_av)]
                if h == 0 and qt == 0:
                    w3.append((s_pe, tok_last_x0big_pe))
                outh_tok = act(lambda a, o=big[:, h * T + qt * 512: h * T + (qt + 1) * 512],
                               i=psav[:, :]: a.activation(o, i, AF.Copy), w3, inc=True)
                psav_free = (s_act, outh_tok)

            for step in range(16 + SKEW):
                if step < 16:
                    p1(step)
                if step >= SKEW:
                    u = step - SKEW
                    p2(u)
                    if units[u][1] == 3:
                        do_av(units[u][0])
        outh_w = [(s_act, outh_tok)]

        if "outh" in tap_d:
            sdma(tap_d["outh"][:, :], big[:, :], s_out, outh_w)
        if "qt" in tap_d:
            sdma(tap_d["qt"][:, :], qT[:, :], s_out, outh_w)
            sdma(tap_d["kt"][:, :], kT[:, :], s_out, outh_w)
            sdma(tap_d["vt"][:, :], vT[:, :], s_out, outh_w)
        if "probs" in tap_d:
            sdma(tap_d["probs"][:, :], probs[:, :], s_out, outh_w)
            sdma(tap_d["probst"][:, :], probsT[:, :], s_out, outh_w)
            sdma(tap_d["stat"][:, :], stat[:, :], s_out, outh_w)

        gateA = [(s_pe, s_pe.n), (s_act, s_act.n), (s_dve, s_dve.n)]
        ctxH.close()

        ctxW = ExitStack()
        ctx.callback(ctxW.close)
        woslab = [ctxW.enter_context(nc.sbuf_tensor(f"woslab{i}", [128, KC * 256], BF16))
                  for i in range(2)]
        x0sl = [ctxW.enter_context(nc.sbuf_tensor(f"x0sl{i}", [128, 512], F32))
                for i in range(4)]
        ysl = ctxW.enter_context(nc.sbuf_tensor("ysl", [128, 4 * 512], F32))
        yrow = [ctxW.enter_context(nc.sbuf_tensor(f"yrow{i}", [128, D], F32))
                for i in range(2)]
        x1fr = ctxW.enter_context(nc.sbuf_tensor("x1fr", [128, 2 * D], F32))
        x1bf = ctxW.enter_context(nc.sbuf_tensor("x1bf", [128, 2 * D], BF16))

        woslabv = [w_.ap().rearrange("p (hc d) -> p hc d", d=256) for w_ in woslab]
        wo_lastuse = [None, None]
        x0_lastuse = [None] * 4
        ysl_lastuse = [None] * 4
        wo_tok = [None, None]

        def load_woslab(dt):
            slot = dt % 2
            w = list(gateA)
            if wo_lastuse[slot]:
                w.append((wo_lastuse[slot][0], wo_lastuse[slot][1], True))
            src = wo_d[:, dt * 256:(dt + 1) * 256]
            gdma(woslabv[slot][:, 0:8], src[0:1024, :].rearrange("(hc p) d -> p hc d", p=128),
                 s_wo, w)
            wo_tok[slot] = gdma(woslabv[slot][:, 8:16],
                                src[1024:2048, :].rearrange("(hc p) d -> p hc d", p=128), s_wo, w)

        load_woslab(0)
        load_woslab(1)
        for dt in range(8):
            slot = dt % 2
            wo_w = [(s_wo, wo_tok[slot])]
            for tg in range(TG):
                sl = tg % 4
                w = list(gateA) + ([x0_lastuse[sl]] if x0_lastuse[sl] else [])
                tok_x0 = sdma(x0sl[sl][:, 0:256],
                              x0_d[tg * 128:(tg + 1) * 128, dt * 256:(dt + 1) * 256],
                              s_x0s[sl], w)
                bank = f"ps{tg % 2}"
                pairs = [(big[:, hc * T + tg * 128: hc * T + tg * 128 + 128],
                          woslabv[slot][:, hc, :]) for hc in range(H)]
                tok_mm = mm_group(ps[tg % 2][:, 0:256], pairs, outh_w + wo_w, bank=bank)
                wo_lastuse[slot] = (s_pe, tok_mm)
                w2 = [(s_pe, tok_mm), (s_x0s[sl], tok_x0)]
                if ysl_lastuse[sl]:
                    w2.append(ysl_lastuse[sl])
                tok_y = dve(lambda v, o=ysl[:, sl * 512:sl * 512 + 256], a=ps[tg % 2][:, 0:256],
                            b=x0sl[sl][:, 0:256]:
                            v.tensor_tensor(o, a, b, ALU.add), w2, inc=True)
                bank_free[bank] = (s_dve, tok_y)
                x0_lastuse[sl] = (s_dve, tok_y)
                tok_st = sdma(ybuf_d[tg * 128:(tg + 1) * 128, dt * 256:(dt + 1) * 256],
                              ysl[:, sl * 512:sl * 512 + 256], s_yw, [(s_dve, tok_y)])
                ysl_lastuse[sl] = (s_yw, tok_st)
            if dt + 2 < 8:
                load_woslab(dt + 2)
        yw_all = [(s_yw, s_yw.n)]

        if "y" in tap_d:
            for tg in range(TG):
                sdma(tap_d["y"][tg * 128:(tg + 1) * 128, :], ybuf_d[tg * 128:(tg + 1) * 128, :],
                     s_out, yw_all)

        yr_lastuse = [None, None]
        x1fr_lastuse = [None, None]
        x1bf_lastuse = [None, None]
        x1T_evac = []
        c0 = 40
        for tg in range(TG):
            sl = tg % 2
            yr = yrow[sl][:, :]
            jnk = x1fr[:, sl * D:(sl + 1) * D]
            w = list(yw_all)
            if yr_lastuse[sl]:
                w += yr_lastuse[sl]
            tok_ld = sdma(yr, ybuf_d[tg * 128:(tg + 1) * 128, :], s_yr[sl], w)
            ld_w = [(s_yr[sl], tok_ld)]
            tok_s1 = dve(lambda v, o=stat[:, c0:c0 + 1], i=yr: v.reduce_sum(o, i, AX.X),
                         ld_w, inc=True)
            tok_ssq = dve(lambda v, o=jnk, a=yr: v.tensor_tensor(o, a, a, ALU.mult),
                          ld_w + ([x1fr_lastuse[sl]] if x1fr_lastuse[sl] else []), inc=True)
            tok_s2 = dve(lambda v, o=stat[:, c0 + 1:c0 + 2], i=jnk: v.reduce_sum(o, i, AX.X),
                         [(s_dve, tok_ssq)], inc=True)
            tok_m = dve(lambda v, o=stat[:, c0 + 2:c0 + 3], i=stat[:, c0:c0 + 1]:
                        v.tensor_scalar(o, i, 1.0 / D, None, ALU.mult),
                        [(s_dve, tok_s1)], inc=True)
            tok_e = dve(lambda v, o=stat[:, c0 + 3:c0 + 4], i=stat[:, c0 + 1:c0 + 2]:
                        v.tensor_scalar(o, i, 1.0 / D, None, ALU.mult),
                        [(s_dve, tok_s2)], inc=True)
            tok_mq = dve(lambda v, o=stat[:, c0 + 4:c0 + 5], a=stat[:, c0 + 2:c0 + 3]:
                         v.tensor_tensor(o, a, a, ALU.mult), [(s_dve, tok_m)], inc=True)
            tok_var = dve(lambda v, o=stat[:, c0 + 3:c0 + 4], a=stat[:, c0 + 3:c0 + 4],
                          b=stat[:, c0 + 4:c0 + 5]: v.tensor_tensor(o, a, b, ALU.subtract),
                          [(s_dve, tok_e), (s_dve, tok_mq)], inc=True)
            act(lambda a, o=stat[:, c0 + 4:c0 + 5], i=stat[:, c0 + 3:c0 + 4],
                b=kcol[:, 1:2]: a.activation(o, i, AF.Ln, bias=b),
                [(s_dve, tok_var)] + bulk_w)
            tok_rstd = act(lambda a, o=stat[:, c0 + 5:c0 + 6], i=stat[:, c0 + 4:c0 + 5]:
                           a.activation(o, i, AF.Exp, scale=-0.5), [], inc=True)
            tok_nmr = dve(lambda v, o=stat[:, c0 + 6:c0 + 7], i=stat[:, c0 + 2:c0 + 3],
                          r=stat[:, c0 + 5:c0 + 6]:
                          v.scalar_tensor_tensor(o, i, -1.0, r, ALU.mult, ALU.mult),
                          [(s_act, tok_rstd), (s_dve, tok_m)], inc=True)
            tok_x1f = dve(lambda v, o=jnk, i=yr, r=stat[:, c0 + 5:c0 + 6], n=stat[:, c0 + 6:c0 + 7]:
                          v.tensor_scalar(o, i, r, n, ALU.mult, ALU.add),
                          [(s_dve, tok_ssq), (s_dve, tok_nmr)], inc=True)
            wbf = [(s_dve, tok_nmr)] + ([x1bf_lastuse[sl]] if x1bf_lastuse[sl] else [])
            tok_x1b = act(lambda a, o=x1bf[:, sl * D:(sl + 1) * D], i=yr,
                          s=stat[:, c0 + 5:c0 + 6], b=stat[:, c0 + 6:c0 + 7]:
                          a.activation(o, i, AF.Identity, scale=s, bias=b), wbf, inc=True)
            yr_lastuse[sl] = [(s_act, tok_x1b), (s_dve, tok_x1f)]
            tok_w = sdma(x1f_d[tg * 128:(tg + 1) * 128, :], jnk, s_x1w, [(s_dve, tok_x1f)])
            x1fr_lastuse[sl] = (s_x1w, tok_w)
            tok_tr = None
            for g8 in range(2):
                for jj in range(8):
                    dc = g8 * 8 + jj
                    w = [(s_act, tok_x1b)]
                    if jj == 0 and ptr_free is not None:
                        w.append(ptr_free)
                    tok_tr = pe(lambda t, o=ptr[:, jj * 128:(jj + 1) * 128],
                                i_=x1bf[:, sl * D + dc * 128: sl * D + (dc + 1) * 128],
                                idn=identb[:, :]: t.transpose(o, i_, idn), w, inc=(jj == 7))
                dst = xTv[:, g8 * 8:(g8 + 1) * 8, tg * 128:(tg + 1) * 128]
                tok_ev = dve(lambda v, d=dst, s=ptrv: v.tensor_copy(d, s),
                             [(s_pe, tok_tr)], inc=True)
                ptr_free = (s_dve, tok_ev)
                x1T_evac.append(tok_ev)
            x1bf_lastuse[sl] = (s_pe, tok_tr)
        x1T_w = [(s_dve, max(x1T_evac))]
        x1w_all = [(s_x1w, s_x1w.n)]

        if "x1t" in tap_d:
            sdma(tap_d["x1t"][:, :], xT[:, :], s_out, x1T_w)
        if "x1" in tap_d:
            for tg in range(TG):
                sdma(tap_d["x1"][tg * 128:(tg + 1) * 128, :], x1f_d[tg * 128:(tg + 1) * 128, :],
                     s_out, x1w_all)

        if upto == "ln1":
            S.op("sync", lambda s, n=s_out.n: s.wait_ge(s_out.h, n), x1w_all + yw_all, None)
            S.check()
            _emit_block(nc, S)
            return nc

        gateL = [(s_pe, s_pe.n), (s_act, s_act.n), (s_dve, s_dve.n)] + list(x1w_all)
        gateB = [(s_pe, s_pe.n), (s_act, s_act.n), (s_dve, s_dve.n)] + list(yw_all) + list(x1w_all)
        wrbv = wrb.ap().rearrange("p (kc e) -> p kc e", e=E)
        tok_rmm = None
        for tg in range(TG):
            pairs = [(xTv[:, kc, tg * 128:(tg + 1) * 128], wrbv[:, kc, :]) for kc in range(KC)]
            ew = x1T_w + bulk_w
            if tg == 0:
                ew = ew + gateB
                if "ps0" in bank_free:
                    ew.append(bank_free["ps0"])
            tok_rmm = mm_group(ps[0][:, tg * 8:(tg + 1) * 8], pairs, ew, bank=None)
        tok_rt = dve(lambda v, o=stat[:, 32:33], i=tempcol[:, :]: v.reciprocal(o, i),
                     bulk_w, inc=True)
        tok_inh = act(lambda a, o=rtr[:, 0:128], i=ps[0][:, 0:128], s=stat[:, 32:33]:
                      a.activation(o, i, AF.Sigmoid, scale=s),
                      [(s_pe, tok_rmm), (s_dve, tok_rt)] + gateB, inc=True)
        bank_free["ps0"] = (s_act, tok_inh)
        dist = ysl[:, 0:128]
        wz = ysl[:, 128:256]
        fbv = ysl[:, 256:384]
        wv_ = ysl[:, 384:512]
        fb2 = ysl[:, 512:640]
        fbw = ysl[:, 640:768]
        t1v = ysl[:, 768:896]
        wsc = gateB + list(yw_all)
        tok_dist = act(lambda a, o=dist, i=rtr[:, 0:128], b=kcol[:, 0:1]:
                       a.activation(o, i, AF.Abs, bias=b),
                       [(s_act, tok_inh)] + wsc + bulk_w, inc=True)
        act(lambda a, o=wz, i=dist: a.activation(o, i, AF.Exp, scale=-10.0), [])
        tok_fb = act(lambda a, o=fbv, i=dist: a.activation(o, i, AF.Exp, scale=float(-1.0 / 0.3)),
                     [], inc=True)
        dve(lambda v, o=t1v, i=rtr[:, 0:128]: v.tensor_scalar(o, i, GOLDEN_LOWER, None, ALU.is_ge),
            [(s_act, tok_fb)] + wsc)
        dve(lambda v, o=wv_, a=wz, b=t1v: v.tensor_tensor(o, a, b, ALU.mult), [])
        dve(lambda v, o=t1v, i=rtr[:, 0:128]: v.tensor_scalar(o, i, GOLDEN_UPPER, None, ALU.is_le), [])
        tok_w = dve(lambda v, o=wv_, a=wv_, b=t1v: v.tensor_tensor(o, a, b, ALU.mult), [], inc=True)
        wtok_toks = []
        r0 = 48
        for tg in range(TG):
            s8 = slice(tg * 8, (tg + 1) * 8)
            t = dve(lambda v, o=stat[:, r0:r0 + 1], i=wv_[:, s8]: v.reduce_sum(o, i, AX.X),
                    [(s_dve, tok_w)], inc=True)
            t = dve(lambda v, o=stat[:, r0:r0 + 1], i=stat[:, r0:r0 + 1]:
                    v.tensor_scalar(o, i, 1e-8, None, ALU.is_lt), [(s_dve, t)], inc=True)
            tok_ne = t
            t = dve(lambda v, o=stat[:, r0 + 1:r0 + 2], i=fbv[:, s8]: v.reduce_max(o, i, AX.X),
                    [], inc=True)
            t = dve(lambda v, o=fb2[:, s8], i=fbv[:, s8], m=stat[:, r0 + 1:r0 + 2]:
                    v.tensor_scalar(o, i, m, None, ALU.is_ge), [(s_dve, t)], inc=True)
            t = dve(lambda v, o=fb2[:, s8], i0=fb2[:, s8], i1=fbv[:, s8]:
                    v.scalar_tensor_tensor(o, i0, -2.0, i1, ALU.mult, ALU.add),
                    [(s_dve, t)], inc=True)
            t = dve(lambda v, o=stat[:, r0 + 1:r0 + 2], i=fb2[:, s8]: v.reduce_max(o, i, AX.X),
                    [(s_dve, t)], inc=True)
            t = dve(lambda v, o=fb2[:, s8], i=fbv[:, s8], m=stat[:, r0 + 1:r0 + 2]:
                    v.tensor_scalar(o, i, m, None, ALU.is_ge), [(s_dve, t)], inc=True)
            t = dve(lambda v, o=fbw[:, s8], a=fbv[:, s8], b=fb2[:, s8]:
                    v.tensor_tensor(o, a, b, ALU.mult), [(s_dve, t)], inc=True)
            t = dve(lambda v, o=stat[:, r0 + 1:r0 + 2], i=fbw[:, s8]: v.reduce_sum(o, i, AX.X),
                    [(s_dve, t)], inc=True)
            t = dve(lambda v, o=stat[:, r0 + 1:r0 + 2], i=stat[:, r0 + 1:r0 + 2]:
                    v.tensor_scalar(o, i, 1e-8, None, ALU.max), [(s_dve, t)], inc=True)
            t = dve(lambda v, o=stat[:, r0 + 2:r0 + 3], i=stat[:, r0 + 1:r0 + 2]:
                    v.reciprocal(o, i), [(s_dve, t)], inc=True)
            t = dve(lambda v, o=fbw[:, s8], i=fbw[:, s8], m=stat[:, r0 + 2:r0 + 3]:
                    v.tensor_scalar(o, i, m, None, ALU.mult), [(s_dve, t)], inc=True)
            t = dve(lambda v, o=t1v[:, s8], a=fbw[:, s8], b=wv_[:, s8]:
                    v.tensor_tensor(o, a, b, ALU.subtract), [(s_dve, t)], inc=True)
            t = dve(lambda v, o=t1v[:, s8], i0=t1v[:, s8], m=stat[:, r0:r0 + 1], i1=wv_[:, s8]:
                    v.scalar_tensor_tensor(o, i0, m, i1, ALU.mult, ALU.add),
                    [(s_dve, t), (s_dve, tok_ne)], inc=True)
            t = dve(lambda v, o=stat[:, r0 + 1:r0 + 2], i=t1v[:, s8]: v.reduce_sum(o, i, AX.X),
                    [(s_dve, t)], inc=True)
            t = dve(lambda v, o=stat[:, r0 + 1:r0 + 2], i=stat[:, r0 + 1:r0 + 2]:
                    v.tensor_scalar(o, i, 1e-8, None, ALU.max), [(s_dve, t)], inc=True)
            t = dve(lambda v, o=stat[:, r0 + 2:r0 + 3], i=stat[:, r0 + 1:r0 + 2]:
                    v.reciprocal(o, i), [(s_dve, t)], inc=True)
            tok_wt = dve(lambda v, o=rtr[:, 160 + tg * 8:160 + (tg + 1) * 8], i=t1v[:, s8],
                         m=stat[:, r0 + 2:r0 + 3]: v.tensor_scalar(o, i, m, None, ALU.mult),
                         [(s_dve, t)], inc=True)
            wtok_toks.append(tok_wt)
        wtok_w = [(s_dve, max(wtok_toks))]

        if "router" in tap_d:
            sdma(tap_d["router"][:, :], rtr[:, 160:288], s_out, wtok_w)

        if upto == "router":
            S.op("sync", lambda s, n=s_out.n: s.wait_ge(s_out.h, n),
                 x1w_all + yw_all + wtok_w, None)
            S.check()
            _emit_block(nc, S)
            return nc

        ctxW.close()
        ctxA.close()
        ctxM = ExitStack()
        ctx.callback(ctxM.close)
        wgu_t = [ctxM.enter_context(nc.sbuf_tensor(f"wgu{i}", [128, 2 * 16 * FT], BF16))
                 for i in range(2)]
        wd_t = [ctxM.enter_context(nc.sbuf_tensor(f"wdt{i}", [128, 8 * DT], BF16))
                for i in range(2)]
        h_t = ctxM.enter_context(nc.sbuf_tensor("h_t", [128, 8 * TT], BF16))
        x1row = ctxM.enter_context(nc.sbuf_tensor("x1row", [128, D], F32))
        out_acc = ctxM.enter_context(nc.sbuf_tensor("out_acc", [128, 8 * D], F32))

        gateC = list(gateL)
        accgate = list(gateL) + list(yw_all) + wtok_w

        def h_cols(fc_local, t0, tn):
            return h_t[:, fc_local * TT + t0: fc_local * TT + t0 + tn]

        def wgu_tile(slot, mat, kc, fb):
            b = mat * 4096 + kc * FT + fb * 128
            return wgu_t[slot][:, b:b + 128]

        def wgu_dst(slot, mat, kc0):
            b = mat * 4096 + kc0 * FT
            return wgu_t[slot].ap()[:, b:b + 8 * FT].rearrange("p (kc f) -> p kc f", f=FT)

        def wd_tile(slot, fc_local):
            return wd_t[slot][:, fc_local * DT: (fc_local + 1) * DT]

        def wd_dst(slot):
            return wd_t[slot].ap()[:, :].rearrange("p (fc d) -> p fc d", d=DT)

        gu_group = 0
        d_group = 0
        wgwu_lastuse = [None, None]
        wd_lastuse = [None, None]
        h_lastuse = None
        x1r_lastuse = None
        acc_started = False

        def load_wgwu(e, half, sidx, slot):
            f0 = half * 1024 + sidx * FT
            w = list(gateC)
            if wgwu_lastuse[slot]:
                w.append((wgwu_lastuse[slot][0], wgwu_lastuse[slot][1], True))
            tok = None
            for mat, md in ((0, wg_d), (1, wu_d)):
                src = md[e * D:(e + 1) * D, f0:f0 + FT]
                gdma(wgu_dst(slot, mat, 0), src[0:1024, :].rearrange("(kc p) f -> p kc f", p=128),
                     s_wgwu[slot], w)
                tok = gdma(wgu_dst(slot, mat, 8),
                           src[1024:2048, :].rearrange("(kc p) f -> p kc f", p=128),
                           s_wgwu[slot], w)
            return tok

        ln2_tails = []
        ln2_state = {"x1_free": None, "acc_out": {}}

        def make_ln2_closures(tile):
            def tail(tg8, tg):
                acc_row = out_acc[:, tg8 * D:(tg8 + 1) * D]
                tok_s1 = dve(lambda v, o=stat[:, c0:c0 + 1], i=x1row[:, :]:
                             v.reduce_sum(o, i, AX.X), [], inc=True)
                tok_ssq = dve(lambda v, o=acc_row, a=x1row[:, :]:
                              v.tensor_tensor(o, a, a, ALU.mult), [], inc=True)
                tok_s2 = dve(lambda v, o=stat[:, c0 + 1:c0 + 2], i=acc_row:
                             v.reduce_sum(o, i, AX.X), [(s_dve, tok_ssq)], inc=True)
                tok_m = dve(lambda v, o=stat[:, c0 + 2:c0 + 3], i=stat[:, c0:c0 + 1]:
                            v.tensor_scalar(o, i, 1.0 / D, None, ALU.mult),
                            [(s_dve, tok_s1)], inc=True)
                tok_e = dve(lambda v, o=stat[:, c0 + 3:c0 + 4], i=stat[:, c0 + 1:c0 + 2]:
                            v.tensor_scalar(o, i, 1.0 / D, None, ALU.mult),
                            [(s_dve, tok_s2)], inc=True)
                tok_mq = dve(lambda v, o=stat[:, c0 + 4:c0 + 5], a=stat[:, c0 + 2:c0 + 3]:
                             v.tensor_tensor(o, a, a, ALU.mult), [(s_dve, tok_m)], inc=True)
                tok_var = dve(lambda v, o=stat[:, c0 + 3:c0 + 4], a=stat[:, c0 + 3:c0 + 4],
                              b=stat[:, c0 + 4:c0 + 5]: v.tensor_tensor(o, a, b, ALU.subtract),
                              [(s_dve, tok_e), (s_dve, tok_mq)], inc=True)
                act(lambda a, o=stat[:, c0 + 4:c0 + 5], i=stat[:, c0 + 3:c0 + 4],
                    b=kcol[:, 1:2]: a.activation(o, i, AF.Ln, bias=b), [(s_dve, tok_var)])
                tok_rstd = act(lambda a, o=stat[:, c0 + 5:c0 + 6], i=stat[:, c0 + 4:c0 + 5]:
                               a.activation(o, i, AF.Exp, scale=-0.5), [], inc=True)
                tok_nmr = dve(lambda v, o=stat[:, c0 + 6:c0 + 7], i=stat[:, c0 + 2:c0 + 3],
                              r=stat[:, c0 + 5:c0 + 6]:
                              v.scalar_tensor_tensor(o, i, -1.0, r, ALU.mult, ALU.mult),
                              [(s_act, tok_rstd), (s_dve, tok_m)], inc=True)
                tok_o = dve(lambda v, o=acc_row, i=x1row[:, :], r=stat[:, c0 + 5:c0 + 6],
                            n=stat[:, c0 + 6:c0 + 7]:
                            v.tensor_scalar(o, i, r, n, ALU.mult, ALU.add),
                            [(s_dve, tok_nmr)], inc=True)
                tok_st = sdma(out_d[tg * 128:(tg + 1) * 128, :], acc_row, s_out,
                              [(s_dve, tok_o)])
                ln2_state["x1_free"] = (s_dve, tok_o)
                ln2_state["acc_out"][tg8] = (s_out, tok_st)

            def load_add(tg8, tg):
                acc_row = out_acc[:, tg8 * D:(tg8 + 1) * D]
                w = list(x1w_all) + list(gateL)
                if ln2_state["x1_free"]:
                    w.append(ln2_state["x1_free"])
                tok_ld = sdma(x1row[:, :], x1f_d[tg * 128:(tg + 1) * 128, :], s_x1r, w)
                dve(lambda v, o=x1row[:, :], b=acc_row:
                    v.tensor_tensor(o, o, b, ALU.add), [(s_x1r, tok_ld)], inc=True)

            def closure(k):
                def run():
                    if k >= 1:
                        tail(k - 1, tile * 8 + k - 1)
                    if k < 8:
                        load_add(k, tile * 8 + k)
                return run
            return [closure(k) for k in range(9)]

        for tile in range(NTILE):
            t0 = tile * TT
            for e in range(E):
                for half in range(NHALF):
                    slab_tok = [load_wgwu(e, half, 0, 0), load_wgwu(e, half, 1, 1)]
                    next_sidx = 2
                    tok_h = None
                    for sidx in range(4):
                        slot = sidx % 2
                        sw = [(s_wgwu[slot], slab_tok[slot])]
                        for fb in range(2):
                            fc_local = sidx * 2 + fb
                            for ts in range(2):
                                gp = gu_group % 2
                                gu_group += 1
                                rhs = [xTv[:, kc, t0 + ts * 512: t0 + (ts + 1) * 512]
                                       for kc in range(KC)]
                                gpairs = [(wgu_tile(slot, 0, kc, fb), rhs[kc]) for kc in range(KC)]
                                upairs = [(wgu_tile(slot, 1, kc, fb), rhs[kc]) for kc in range(KC)]
                                ew = sw + x1T_w
                                if tile == 0 and e == 0 and half == 0:
                                    ew = ew + gateC
                                tok_g = mm_group(ps[gp][:, :], gpairs, ew, bank=f"ps{gp}")
                                tok_u = mm_group(ps[2 + gp][:, :], upairs, [], bank=f"ps{2 + gp}")
                                wgwu_lastuse[slot] = (s_pe, tok_u)
                                hsl = h_cols(fc_local, ts * 512, 512)
                                hw = [h_lastuse] if h_lastuse else list(gateL)
                                tok_si = act(lambda a, o=hsl, i=ps[gp][:, :]:
                                             a.activation(o, i, AF.Silu),
                                             [(s_pe, tok_g)] + hw, inc=True)
                                bank_free[f"ps{gp}"] = (s_act, tok_si)
                                tok_h = dve(lambda v, o=hsl, i=ps[2 + gp][:, :]:
                                            v.tensor_tensor(o, o, i, ALU.mult),
                                            [(s_pe, tok_u), (s_act, tok_si)], inc=True)
                                bank_free[f"ps{2 + gp}"] = (s_dve, tok_h)
                        if next_sidx < 4:
                            slab_tok[next_sidx % 2] = load_wgwu(e, half, next_sidx, next_sidx % 2)
                            next_sidx += 1
                        for _ in range(3):
                            if ln2_tails:
                                ln2_tails.pop(0)()
                    h_ready = (s_dve, tok_h)
                    tok_d = None
                    for dt in range(4):
                        slot = dt % 2
                        w = list(gateC) + ([wd_lastuse[slot]] if wd_lastuse[slot] else [])
                        src = wd_d[e * D + half * 1024: e * D + (half + 1) * 1024,
                                   dt * 512:(dt + 1) * 512]
                        tok_wd = gdma(wd_dst(slot), src.rearrange("(fc p) d -> p fc d", p=128),
                                      s_wd[slot], w)
                        for tg8 in range(8):
                            dp = d_group % 2
                            d_group += 1
                            pairs = [(h_cols(fc, tg8 * 128, 128), wd_tile(slot, fc))
                                     for fc in range(8)]
                            tok_d = mm_group(ps[4 + dp][:, :], pairs,
                                             [h_ready, (s_wd[slot], tok_wd)], bank=f"ps{4 + dp}")
                            wd_lastuse[slot] = (s_pe, tok_d)
                            acc_sl = out_acc[:, tg8 * D + dt * 512: tg8 * D + (dt + 1) * 512]
                            wcol = rtr[:, 160 + (tile * 8 + tg8) * 8 + e:
                                       161 + (tile * 8 + tg8) * 8 + e]
                            ew2 = [(s_pe, tok_d)] + wtok_w
                            if not acc_started:
                                ew2 = ew2 + accgate
                            if e == 0 and half == 0 and tg8 in ln2_state["acc_out"]:
                                ew2 = ew2 + [ln2_state["acc_out"][tg8]]
                            if e == 0 and half == 0:
                                tok_ac = dve(lambda v, o=acc_sl, i=ps[4 + dp][:, :], m=wcol:
                                             v.tensor_scalar(o, i, m, None, ALU.mult),
                                             ew2, inc=True)
                            else:
                                tok_ac = dve(lambda v, o=acc_sl, i=ps[4 + dp][:, :], m=wcol:
                                             v.scalar_tensor_tensor(o, i, m, o, ALU.mult, ALU.add),
                                             ew2, inc=True)
                            acc_started = True
                            bank_free[f"ps{4 + dp}"] = (s_dve, tok_ac)
                    h_lastuse = (s_pe, tok_d)
            ln2_tails.extend(make_ln2_closures(tile))

        while ln2_tails:
            ln2_tails.pop(0)()

        S.op("sync", lambda s, n=s_out.n: s.wait_ge(s_out.h, n), (), None)
        S.check()
        _emit_block(nc, S)
    return nc


def _emit_block(nc, S):
    with nc.Block() as block:
        @block.tensor
        def _(t):
            S.replay("tensor", t)

        @block.vector
        def _(v):
            S.replay("vector", v)

        @block.scalar
        def _(sc):
            S.replay("scalar", sc)

        @block.gpsimd
        def _(g):
            S.replay("gpsimd", g)

        @block.sync
        def _(sy):
            S.replay("sync", sy)


def kernel(L0_attn_hidden_states, L0_attn_wq, L0_attn_wk, L0_attn_wv, L0_attn_wo,
           L0_ffn_Wg, L0_ffn_Wu, L0_ffn_Wd, L0_ffn_Wr, L0_ffn_temperature):
    x = np.ascontiguousarray(np.asarray(L0_attn_hidden_states, dtype=np.float32))
    base = {
        "wq": np.ascontiguousarray(np.asarray(L0_attn_wq, dtype=np.float32)),
        "wk": np.ascontiguousarray(np.asarray(L0_attn_wk, dtype=np.float32)),
        "wv": np.ascontiguousarray(np.asarray(L0_attn_wv, dtype=np.float32)),
        "wo": np.ascontiguousarray(np.asarray(L0_attn_wo, dtype=np.float32)),
        "wg3": np.ascontiguousarray(np.asarray(L0_ffn_Wg, dtype=np.float32).reshape(E * D, D)),
        "wu3": np.ascontiguousarray(np.asarray(L0_ffn_Wu, dtype=np.float32).reshape(E * D, D)),
        "wd3": np.ascontiguousarray(np.asarray(L0_ffn_Wd, dtype=np.float32).reshape(E * D, D)),
        "wr": np.ascontiguousarray(np.asarray(L0_ffn_Wr, dtype=np.float32)),
        "tempcol": np.broadcast_to(
            np.asarray(L0_ffn_temperature, dtype=np.float32).reshape(1, 1), (128, 1)).copy(),
    }
    cos2, sin2, sgn, ident, maskt = _host_consts()
    kcol = np.zeros((128, 4), np.float32)
    kcol[:, 0] = -GOLDEN_CENTER
    kcol[:, 1] = LN_EPS
    base.update({"cos2": np.asarray(cos2), "sin2": np.asarray(sin2), "sgncol": sgn,
                 "identb": np.asarray(ident), "maskt": np.asarray(maskt), "kcol": kcol})
    nc = build_nc()
    in_maps = [dict(base, x0=np.ascontiguousarray(x[b])) for b in range(NCORE)]
    res = run_bass_kernel_spmd(nc, in_maps, core_ids=list(range(NCORE)))
    return np.stack([res.results[b]["out"] for b in range(NCORE)], axis=0)
